# version 13
# baseline (speedup 1.0000x reference)
import math
import contextlib
import numpy as np
import ml_dtypes
import concourse.bass as bass
import concourse.mybir as mybir
from concourse.bass_utils import run_bass_kernel_spmd

F32 = mybir.dt.float32
BF16 = mybir.dt.bfloat16
AF = mybir.ActivationFunctionType
ALU = mybir.AluOpType
AX = mybir.AxisListType.X
bfnp = ml_dtypes.bfloat16

T, HID = 8192, 2048
H, DK, DV, M, KW = 16, 128, 128, 128, 4
SCALE = DK ** -0.5
NCORES = 8
CHUNK = 128
SEG = 1024
NSEG = T // SEG
CPS = SEG // CHUNK
DBG_PROJ = True
DBG_PARTS = 31
DBG_SCAN = True

ENGINES = ("tensor", "vector", "scalar", "gpsimd", "sync")


class Res:
    __slots__ = ("w", "readers", "excl")

    def __init__(self, excl=False):
        self.w = None
        self.readers = {}
        self.excl = excl

    def rd(self):
        return [self.w] if self.w is not None else []

    def wr(self):
        out = [self.w] if self.w is not None else []
        out.extend(self.readers.values())
        return out


class Prog:

    NLANE = 4

    def __init__(self, nc):
        self.nc = nc
        self.ops = {e: [] for e in ENGINES}
        self.dma_lane = {e: {} for e in ENGINES}

    def op(self, eng, fn, reads=(), writes=(), extra=(), dma=False):
        reads = tuple(reads)
        writes = tuple(writes) + tuple(r for r in reads if r.excl)
        reads = tuple(r for r in reads if not r.excl)
        waits = []
        for r in reads:
            waits.extend(r.rd())
        for r in writes:
            waits.extend(r.wr())
        waits.extend(t for t in extra if t is not None)
        best = {}
        for (pe, ps) in waits:
            if pe == eng == "tensor":
                continue
            lane = self.dma_lane[pe].get(ps)
            key = (pe, "c") if lane is None else (pe, "d%d" % (lane % self.NLANE))
            if best.get(key, (None, -1))[1] < ps:
                best[key] = (pe, ps)
        self.ops[eng].append((fn, sorted(best.values()), 16 if dma else 1))
        tok = (eng, len(self.ops[eng]) - 1)
        if dma:
            self.dma_lane[eng][tok[1]] = len(self.dma_lane[eng])
        for r in reads:
            r.readers[eng] = tok
        for r in writes:
            r.w = tok
            r.readers = {}
        return tok

    def barrier(self):
        toks = [(e, len(self.ops[e]) - 1) for e in ENGINES if self.ops[e]]
        for e in ENGINES:
            if self.ops[e]:
                self.op(e, lambda E: E.nop(), extra=list(toks))

    def dma(self, eng, out, in_, reads=(), writes=(), extra=(), **kw):
        return self.op(eng, lambda E: E.dma_start(out=out, in_=in_, **kw),
                       reads=reads, writes=writes, extra=extra, dma=True)

    def finalize(self, stack):
        nc = self.nc
        used = {e: set() for e in ENGINES}
        for e in ENGINES:
            for i, (_, waits, amt) in enumerate(self.ops[e]):
                if amt == 16:
                    used[e].add(i)
                for pe, ps in waits:
                    used[pe].add(ps)
        NLANE = self.NLANE
        sems = {}
        val_at = {}
        for e in ENGINES:
            cum = {"c": 0}
            for l in range(NLANE):
                cum[f"d{l}"] = 0
            va = {}
            incs = used[e]
            kinds = set()
            for i, (_, _, amt) in enumerate(self.ops[e]):
                if amt == 16:
                    kind = f"d{self.dma_lane[e][i] % NLANE}"
                else:
                    kind = "c"
                if i in incs:
                    cum[kind] += amt
                    kinds.add(kind)
                va[i] = (kind, cum[kind])
            val_at[e] = va
            for kind in kinds:
                sems[(e, kind)] = stack.enter_context(nc.semaphore(f"ms_{e}_{kind}"))
        block = stack.enter_context(nc.Block())

        def make_runner(e):
            ops = self.ops[e]
            incs = used[e]
            va = val_at[e]

            def run(E):
                waited = {}
                prev_lane = {}
                for i, (fn, waits, amt) in enumerate(ops):
                    if amt == 16:
                        kind, v = va[i]
                        pv = prev_lane.get(kind, 0)
                        if pv > waited.get((e, kind), 0):
                            E.wait_ge(sems[(e, kind)], pv)
                            waited[(e, kind)] = pv
                    for pe, ps in waits:
                        kind2, v2 = val_at[pe][ps]
                        if waited.get((pe, kind2), 0) >= v2:
                            continue
                        E.wait_ge(sems[(pe, kind2)], v2)
                        waited[(pe, kind2)] = v2
                    inst = fn(E)
                    if i in incs:
                        inst.then_inc(sems[(e, va[i][0])], amt)
                    if amt == 16:
                        prev_lane[va[i][0]] = va[i][1]
            return run

        for e in ENGINES:
            if self.ops[e]:
                getattr(block, e)(make_runner(e))
        return sems, val_at


def ceil_div(a, b):
    return (a + b - 1) // b



def build_A(ctx=None):
    if ctx is None:
        nc = bass.Bass("TRN2", target_bir_lowering=False)
    else:
        nc = ctx["nc"]
    x = nc.dram_tensor("x", [T, HID], F32, kind="ExternalInput").ap()
    wq = nc.dram_tensor("wq", [HID, 256], F32, kind="ExternalInput").ap()
    wk = nc.dram_tensor("wk", [HID, 256], F32, kind="ExternalInput").ap()
    wv = nc.dram_tensor("wv", [HID, 256], F32, kind="ExternalInput").ap()
    ww = nc.dram_tensor("ww", [HID, 256], F32, kind="ExternalInput").ap()
    cq = nc.dram_tensor("cq", [128, 8], F32, kind="ExternalInput").ap()
    ck = nc.dram_tensor("ck", [128, 8], F32, kind="ExternalInput").ap()
    cv = nc.dram_tensor("cv", [128, 8], F32, kind="ExternalInput").ap()
    wf1 = nc.dram_tensor("wf1", [HID, 128], F32, kind="ExternalInput").ap()
    wf2 = nc.dram_tensor("wf2", [128, 256], F32, kind="ExternalInput").ap()
    wg1 = nc.dram_tensor("wg1", [HID, 128], F32, kind="ExternalInput").ap()
    wg2 = nc.dram_tensor("wg2", [128, 256], F32, kind="ExternalInput").ap()
    wbp = nc.dram_tensor("wbp", [128, 32], F32, kind="ExternalInput").ap()
    bg2 = nc.dram_tensor("bg2", [128, 2], F32, kind="ExternalInput").ap()
    normw = nc.dram_tensor("normw", [128, 1], F32, kind="ExternalInput").ap()
    if ctx is None:
        o_fm = nc.dram_tensor("o_fm", [256, T], BF16, kind="ExternalOutput").ap()
        o_dst = lambda s, cc: o_fm[cc * 128:(cc + 1) * 128, s * SEG:(s + 1) * SEG]
    else:
        o_dst = ctx["o_dst"]
    xbf = nc.dram_tensor("xbf", [T, HID], BF16, kind="Internal").ap()

    triu = np.triu(np.ones((CHUNK, CHUNK), dtype=np.float32) * SCALE * 0 + 1.0)
    triu_h = nc.inline_tensor(triu.astype(bfnp), name="a_triu")
    ident_h = nc.inline_tensor(np.eye(128, dtype=bfnp), name="a_ident")
    ones_h = nc.inline_tensor(np.ones((128, 1), dtype=bfnp), name="a_ones_col")
    onesr_h = nc.inline_tensor(np.ones((1, 128), dtype=bfnp), name="a_ones_row")

    with contextlib.ExitStack() as st:
        sb = lambda name, shape, dt: st.enter_context(nc.sbuf_tensor("a_" + name, shape, dt))
        ps = lambda name, shape, dt: st.enter_context(nc.psum_tensor("a_" + name, shape, dt))

        wq_sb = sb("wq_sb", [128, 16 * 256], BF16)
        wk_sb = sb("wk_sb", [128, 16 * 256], BF16)
        wv_sb = sb("wv_sb", [128, 16 * 256], BF16)
        ww_sb = sb("ww_sb", [128, 16 * 256], BF16)
        wf1_sb = sb("wf1_sb", [128, 16 * 128], BF16)
        wg1_sb = sb("wg1_sb", [128, 16 * 128], BF16)
        wf2_sb = sb("wf2_sb", [128, 256], BF16)
        wg2_sb = sb("wg2_sb", [128, 256], BF16)
        wb_sb = sb("wb_sb", [128, 32], BF16)
        cq_sb = sb("cq_sb", [128, 8], F32)
        ck_sb = sb("ck_sb", [128, 8], F32)
        cv_sb = sb("cv_sb", [128, 8], F32)
        bg2_sb = sb("bg2_sb", [128, 2], F32)
        normw_sb = sb("normw_sb", [128, 1], F32)
        triu_sb = sb("triu_sb", [128, 128], BF16)
        id_sb = sb("id_sb", [128, 128], BF16)
        ones_sb = sb("ones_sb", [128, 1], BF16)
        onesr_sb = sb("onesr_sb", [1, 128], BF16)

        xt = sb("xt", [128, 16 * SEG], BF16)
        qT = sb("qT", [128, 2 * SEG], BF16)
        kT = sb("kT", [128, 2 * SEG], BF16)
        k_tm = sb("k_tm", [128, 2 * SEG], BF16)
        v_tm = sb("v_tm", [128, 2 * SEG], BF16)
        uT = sb("uT", [128, 2 * SEG], BF16)
        u_tm = sb("u_tm", [128, 2 * SEG], BF16)
        cumc = sb("cumc", [128, 2 * SEG], F32)
        Gt = sb("Gt", [128, 2 * SEG], BF16)
        o_seg = sb("o_seg", [128, 2 * SEG], BF16)
        beta0 = sb("beta0", [1, 512], F32)
        beta1 = sb("beta1", [1, 512], F32)
        refc = sb("refc", [128, 2 * CPS], F32)
        lastc = sb("lastc", [128, 2 * CPS], F32)
        eref = sb("eref", [128, 2 * CPS], F32)
        elmr = sb("elmr", [128, 2 * CPS], F32)
        glast = sb("glast", [128, 2 * CPS], F32)

        pre_b = [sb(f"pre{i}", [128, 2 * (512 + 3)], F32) for i in range(2)]
        halo = sb("halo", [128, 2 * 4 * 3], F32)
        f1_bf = sb("f1_bf", [128, 512], BF16)
        wtmp_b = [sb(f"wtmp{i}", [128, 2 * 512], F32) for i in range(2)]
        sq_bf = sb("sq_bf", [128, 512], BF16)
        sq_bf2 = sb("sq_bf2", [128, 512], BF16)
        row_a = sb("row_a", [1, 512], F32)
        row_b = sb("row_b", [1, 512], F32)
        row_c = sb("row_c", [1, 512], BF16)
        g_abs = sb("g_abs", [128, 512], F32)
        g_ln = sb("g_ln", [128, 512], F32)
        g_min = sb("g_min", [128, 512], F32)

        SkT = [sb(f"SkT{h}", [128, 2 * 128], F32) for h in range(2)]
        SvT = [sb(f"SvT{h}", [128, 2 * 128], F32) for h in range(2)]
        Sk_dm = [sb(f"Skdm{h}", [128, 128], BF16) for h in range(2)]
        SvT_sc = [sb(f"Svsc{h}", [128, 128], BF16) for h in range(2)]
        atm = [sb(f"atm{h}", [128, 128], BF16) for h in range(2)]
        btm = [sb(f"btm{h}", [128, 128], BF16) for h in range(2)]
        at_c = [sb(f"at_c{h}", [128, 128], BF16) for h in range(2)]
        atT = [sb(f"atT{h}", [128, 128], BF16) for h in range(2)]
        s_sb = [sb(f"s_sb{h}", [128, 128], F32) for h in range(2)]
        e_sb = [sb(f"e_sb{h}", [128, 128], F32) for h in range(2)]
        gres = sb("gres", [128, 2 * SEG], BF16)
        nmax = [sb(f"nmax{h}", [128, 1], F32) for h in range(2)]
        rsum = [sb(f"rsum{h}", [128, 1], F32) for h in range(2)]
        rinv = [sb(f"rinv{h}", [128, 1], F32) for h in range(2)]
        orow = [sb(f"orow{h}", [1, 128], F32) for h in range(2)]
        orow2 = [sb(f"orow2{h}", [1, 128], F32) for h in range(2)]
        orow_bf = [sb(f"orow_bf{h}", [1, 128], BF16) for h in range(2)]
        osq = [sb(f"osq{h}", [128, 128], BF16) for h in range(2)]
        o1 = [sb(f"o1{h}", [128, 128], F32) for h in range(2)]
        tsk = [sb(f"tsk{h}", [128, 128], F32) for h in range(2)]
        tsv = [sb(f"tsv{h}", [128, 128], F32) for h in range(2)]
        vcol = sb("vcol", [128, 1], F32)
        vcol2 = sb("vcol2", [128, 1], F32)

        bkA = [ps(f"bkA{h}", [128, 512], F32) for h in range(2)]
        bkB = [ps(f"bkB{h}", [128, 512], F32) for h in range(2)]
        bkT = [ps(f"bkT{h}", [128, 256], BF16) for h in range(2)]
        pA, p1 = bkA[0], bkA[1]
        pB = bkB[0]
        pR = bkB[1]
        pTf = bkB[0][:, 0:128]

        P = Prog(nc) if ctx is None else ctx["P"]
        rx = {name: Res() for name in (
            "wq", "wk", "wv", "ww", "wf1", "wg1", "wf2", "wg2", "wb", "cq",
            "ck", "cv", "bg2", "normw", "triu", "id", "ones", "onesr",
            "xt", "qT", "kT", "k_tm", "v_tm", "uT", "u_tm", "cumc", "Gt",
            "o_seg", "beta0", "beta1", "refc", "lastc", "eref", "elmr",
            "glast", "pre0", "pre1", "halo", "f1", "wtmp0", "wtmp1", "sq", "rowa", "rowb",
            "rowc", "gabs", "gln", "gmin",
            "gres", "vcol", "vcol2", "xbf",
        )}
        for _h in range(2):
            for _n in ("atm", "btm", "atc", "atT", "s", "e", "nmax", "rsum",
                       "rinv", "orow", "orow2", "orowbf", "osq", "o1", "tsk", "tsv"):
                rx[f"{_n}{_h}"] = Res()
        for n in ("bkA0", "bkA1", "bkB0", "bkB1", "bkT0", "bkT1"):
            rx[n] = Res(excl=True)
        rx["pA"] = rx["bkA0"]; rx["p1"] = rx["bkA1"]
        rx["pB"] = rx["pTf"] = rx["bkB0"]; rx["pR"] = rx["bkB1"]
        for h in range(2):
            rx[f"SkT{h}"] = Res(); rx[f"SvT{h}"] = Res()
            rx[f"Skdm{h}"] = Res(); rx[f"Svsc{h}"] = Res()
        R = lambda n: rx[n]

        def wload(dst, src, res, kk):
            P.dma("gpsimd", dst[:].rearrange("p (k c) -> p k c", k=kk),
                  src.rearrange("(k p) c -> p k c", p=128), writes=[R(res)])
        wload(wq_sb, wq, "wq", 16)
        wload(wk_sb, wk, "wk", 16)
        wload(wv_sb, wv, "wv", 16)
        wload(ww_sb, ww, "ww", 16)
        wload(wf1_sb, wf1, "wf1", 16)
        wload(wg1_sb, wg1, "wg1", 16)
        P.dma("gpsimd", wf2_sb[:], wf2[:, :], writes=[R("wf2")])
        P.dma("gpsimd", wg2_sb[:], wg2[:, :], writes=[R("wg2")])
        P.dma("gpsimd", wb_sb[:], wbp[:, :], writes=[R("wb")])
        P.dma("gpsimd", cq_sb[:], cq[:, :], writes=[R("cq")])
        P.dma("gpsimd", ck_sb[:], ck[:, :], writes=[R("ck")])
        P.dma("gpsimd", cv_sb[:], cv[:, :], writes=[R("cv")])
        P.dma("gpsimd", bg2_sb[:], bg2[:, :], writes=[R("bg2")])
        P.dma("gpsimd", normw_sb[:], normw[:, :], writes=[R("normw")])
        P.dma("gpsimd", triu_sb[:], triu_h.ap()[:, :], writes=[R("triu")])
        P.dma("gpsimd", id_sb[:], ident_h.ap()[:, :], writes=[R("id")])
        P.dma("gpsimd", ones_sb[:], ones_h.ap()[:, :], writes=[R("ones")])
        P.dma("gpsimd", onesr_sb[:], onesr_h.ap()[:, :], writes=[R("onesr")])
        for h in range(2):
            P.op("vector", lambda E, t=SkT[h]: E.memset(t[:], 0.0), writes=[R(f"SkT{h}")])
            P.op("vector", lambda E, t=SvT[h]: E.memset(t[:], 0.0), writes=[R(f"SvT{h}")])
            P.op("vector", lambda E, t=Sk_dm[h]: E.memset(t[:], 0.0), writes=[R(f"Skdm{h}")])
            P.op("vector", lambda E, t=SvT_sc[h]: E.memset(t[:], 0.0), writes=[R(f"Svsc{h}")])
        P.op("vector", lambda E: E.memset(halo[:], 0.0), writes=[R("halo")])
        for h in range(2):
            P.op("vector", lambda E, t=btm[h]: E.memset(t[:], 0.0), writes=[R(f"btm{h}")])

        xbf_tok = []
        for s in range(NSEG):
            t = P.dma("gpsimd", xbf[s * SEG:(s + 1) * SEG, :], x[s * SEG:(s + 1) * SEG, :],
                      writes=[R("xbf")])
            xbf_tok.append(t)

        out_toks = []

        def proj_tile(s, t, tok0, TL):
            xs = lambda k: xt[:, k * SEG + t * 512: k * SEG + t * 512 + TL]
            pidx = [0]
            for prji, (name, wsb, wres, csb, cres) in enumerate(() if not (DBG_PARTS & 1) else (
                    ("q", wq_sb, "wq", cq_sb, "cq"), ("k", wk_sb, "wk", ck_sb, "ck"),
                    ("v", wv_sb, "wv", cv_sb, "cv"), ("w", ww_sb, "ww", cv_sb, "cv"))):
                pre = pre_b[prji % 2]
                wtmp = wtmp_b[prji % 2]
                rpre = "pre%d" % (prji % 2)
                rwt = "wtmp%d" % (prji % 2)
                for cc in range(2):
                    pb, pn = (pA, "pA") if pidx[0] % 2 == 0 else (p1, "p1")
                    pidx[0] += 1
                    for k in range(16):
                        P.op("tensor",
                             lambda E, pre=pre, wtmp=wtmp, k=k, cc=cc, wsb=wsb, xs=xs, pb=pb: E.matmul(
                                 pb[:, :TL], wsb[:, k * 256 + cc * 128: k * 256 + cc * 128 + 128],
                                 xs(k), start=(k == 0), stop=(k == 15)),
                             reads=[R(wres), R("xt")],
                             writes=[R(pn)])
                    pcol = cc * 515
                    P.op("vector",
                         lambda E, pre=pre, wtmp=wtmp, pcol=pcol, pb=pb: E.tensor_copy(pre[:, pcol + 3: pcol + 3 + TL], pb[:, :TL]),
                         reads=[R(pn)], writes=[R(rpre)])
                pi = {"q": 0, "k": 1, "v": 2, "w": 3}[name]
                for cc in range(2):
                    pcol = cc * 515
                    P.op("gpsimd",
                         lambda E, pre=pre, wtmp=wtmp, pcol=pcol, pi=pi, cc=cc: E.tensor_copy(
                             pre[:, pcol:pcol + 3], halo[:, (cc * 4 + pi) * 3:(cc * 4 + pi) * 3 + 3]),
                         reads=[R("halo")], writes=[R(rpre)])
                    wk0 = cc * 512
                    for i in range(KW):
                        sh = 3 - (KW - 1 - i)
                        if i == 0:
                            P.op("vector",
                                 lambda E, pre=pre, wtmp=wtmp, pcol=pcol, sh=sh, wk0=wk0, csb=csb, cc=cc, i=i:
                                 E.tensor_scalar(wtmp[:, wk0:wk0 + TL],
                                                 pre[:, pcol + sh:pcol + sh + TL],
                                                 csb[:, cc * 4 + i:cc * 4 + i + 1], None, ALU.mult),
                                 reads=[R(rpre), R(cres)], writes=[R(rwt)])
                        else:
                            P.op("vector",
                                 lambda E, pre=pre, wtmp=wtmp, pcol=pcol, sh=sh, wk0=wk0, csb=csb, cc=cc, i=i:
                                 E.scalar_tensor_tensor(wtmp[:, wk0:wk0 + TL],
                                                        pre[:, pcol + sh:pcol + sh + TL],
                                                        csb[:, cc * 4 + i:cc * 4 + i + 1],
                                                        wtmp[:, wk0:wk0 + TL], ALU.mult, ALU.add),
                                 reads=[R(rpre), R(cres)], writes=[R(rwt)])
                    P.op("gpsimd",
                         lambda E, pre=pre, wtmp=wtmp, pcol=pcol, pi=pi, cc=cc: E.tensor_copy(
                             halo[:, (cc * 4 + pi) * 3:(cc * 4 + pi) * 3 + 3],
                             pre[:, pcol + TL:pcol + TL + 3]),
                         reads=[R(rpre)], writes=[R("halo")])
                    dst_map = {"q": (qT, "qT"), "k": (kT, "kT")}
                    if name in ("q", "k"):
                        dsb, dres = dst_map[name]
                        P.op("scalar",
                             lambda E, pre=pre, wtmp=wtmp, dsb=dsb, cc=cc, wk0=wk0: E.activation(
                                 out=dsb[:, cc * SEG + tok0:cc * SEG + tok0 + TL],
                                 in_=wtmp[:, wk0:wk0 + TL], func=AF.Silu),
                             reads=[R(rwt)], writes=[R(dres)])
                    elif name == "v":
                        P.op("scalar",
                             lambda E, pre=pre, wtmp=wtmp, cc=cc, wk0=wk0: E.activation(
                                 out=sq_bf[:, :TL], in_=wtmp[:, wk0:wk0 + TL], func=AF.Silu),
                             reads=[R(rwt)], writes=[R("sq")])
                        for b in range(TL // 128):
                            bt, btn = bkT[cc], f"bkT{cc}"
                            P.op("tensor",
                                 lambda E, b=b, bt=bt: E.transpose(bt[:, 0:128], sq_bf[:, b * 128:(b + 1) * 128], id_sb[:]),
                                 reads=[R("sq"), R("id")], writes=[R(btn)])
                            P.op("vector",
                                 lambda E, b=b, cc=cc, bt=bt: E.tensor_copy(
                                     v_tm[:, cc * SEG + tok0 + b * 128: cc * SEG + tok0 + b * 128 + 128], bt[:, 0:128]),
                                 reads=[R(btn)], writes=[R("v_tm")])
                    else:
                        P.op("scalar",
                             lambda E, pre=pre, wtmp=wtmp, cc=cc, wk0=wk0: E.activation(
                                 out=wtmp[:, wk0:wk0 + TL], in_=wtmp[:, wk0:wk0 + TL], func=AF.Silu),
                             reads=[R(rwt)], writes=[R(rwt)])

            for k in range(16 if DBG_PARTS & 2 else 0):
                P.op("tensor",
                     lambda E, k=k, xs=xs: E.matmul(pA[:, :TL], wf1_sb[:, k * 128:(k + 1) * 128],
                                                    xs(k), start=(k == 0), stop=(k == 15)),
                     reads=[R("wf1"), R("xt")], writes=[R("pA")])
            if DBG_PARTS & 2:
              P.op("vector", lambda E: E.tensor_copy(f1_bf[:, :TL], pA[:, :TL]),
                 reads=[R("pA")], writes=[R("f1")])
            for cc in range(2 if DBG_PARTS & 2 else 0):
                P.op("tensor",
                     lambda E, cc=cc: E.matmul(p1[:, :TL], wf2_sb[:, cc * 128:(cc + 1) * 128],
                                               f1_bf[:, :TL], start=True, stop=True),
                     reads=[R("wf2"), R("f1")], writes=[R("p1")])
                P.op("scalar", lambda E: E.activation(out=g_abs[:, :TL], in_=p1[:, :TL], func=AF.Abs),
                     reads=[R("p1")], writes=[R("gabs")])
                P.op("scalar", lambda E: E.activation(out=g_ln[:, :TL], in_=g_abs[:, :TL],
                                                      func=AF.Exp, scale=-1.0),
                     reads=[R("gabs")], writes=[R("gln")])
                P.op("scalar", lambda E: E.activation(out=g_ln[:, :TL], in_=g_ln[:, :TL],
                                                      func=AF.Ln, bias=biasc[:, 2:3]),
                     reads=[R("gln"), R("biasc")], writes=[R("gln")])
                P.op("vector", lambda E: E.tensor_single_scalar(g_min[:, :TL], p1[:, :TL], 0.0, ALU.min),
                     reads=[R("p1")], writes=[R("gmin")])
                P.op("vector", lambda E: E.tensor_sub(g_abs[:, :TL], g_min[:, :TL], g_ln[:, :TL]),
                     reads=[R("gmin"), R("gln")], writes=[R("gabs")])
                for b in range(TL // 128):
                    c0 = cc * SEG + tok0 + b * 128
                    P.op("vector",
                         lambda E, b=b, c0=c0: E.tensor_tensor_scan(
                             cumc[:, c0:c0 + 128], g_abs[:, b * 128:(b + 1) * 128],
                             g_abs[:, b * 128:(b + 1) * 128], 0.0, ALU.add, ALU.bypass),
                         reads=[R("gabs")], writes=[R("cumc")])

            for hh, (bres, bsb) in enumerate((() if not (DBG_PARTS & 4) else (("beta0", beta0), ("beta1", beta1)))):
                for k in range(16):
                    P.op("tensor",
                         lambda E, k=k, hh=hh, xs=xs: E.matmul(
                             pB[0:1, :TL], wb_sb[:, hh * 16 + k:hh * 16 + k + 1], xs(k),
                             start=(k == 0), stop=(k == 15)),
                         reads=[R("wb"), R("xt")], writes=[R("pB")])
                P.op("scalar",
                     lambda E, bsb=bsb: E.activation(out=bsb[:, :TL], in_=pB[0:1, :TL],
                                                     func=AF.Sigmoid),
                     reads=[R("pB")], writes=[R(bres)])

            for k in range(16 if DBG_PARTS & 8 else 0):
                P.op("tensor",
                     lambda E, k=k, xs=xs: E.matmul(pA[:, :TL], wg1_sb[:, k * 128:(k + 1) * 128],
                                                    xs(k), start=(k == 0), stop=(k == 15)),
                     reads=[R("wg1"), R("xt")], writes=[R("pA")])
            if DBG_PARTS & 8:
              P.op("vector", lambda E: E.tensor_copy(f1_bf[:, :TL], pA[:, :TL]),
                 reads=[R("pA")], writes=[R("f1")])
            for cc in range(2 if DBG_PARTS & 8 else 0):
                P.op("tensor",
                     lambda E, cc=cc: E.matmul(p1[:, :TL], wg2_sb[:, cc * 128:(cc + 1) * 128],
                                               f1_bf[:, :TL], start=True, stop=True),
                     reads=[R("wg2"), R("f1")], writes=[R("p1")])
                P.op("scalar",
                     lambda E, cc=cc: E.activation(out=g_min[:, :TL], in_=p1[:, :TL],
                                                   func=AF.Sigmoid, bias=bg2_sb[:, cc:cc + 1]),
                     reads=[R("p1"), R("bg2")], writes=[R("gmin")])
                P.op("vector",
                     lambda E, cc=cc: E.tensor_scalar(Gt[:, cc * SEG + tok0:cc * SEG + tok0 + TL],
                                                      g_min[:, :TL], normw_sb[:, 0:1], None, ALU.mult),
                     reads=[R("gmin"), R("normw")], writes=[R("Gt")])

            for cc in range(2 if DBG_PARTS & 16 else 0):
                wk0 = cc * 512
                brow = beta0 if cc == 0 else beta1
                bres = "beta0" if cc == 0 else "beta1"
                P.op("vector", lambda E, wk0=wk0: E.tensor_mul(sq_bf[:, :TL], wtmp_b[1][:, wk0:wk0 + TL],
                                                               wtmp_b[1][:, wk0:wk0 + TL]),
                     reads=[R("wtmp1")], writes=[R("sq")])
                P.op("tensor", lambda E: E.matmul(pB[0:1, :TL], ones_sb[:], sq_bf[:, :TL],
                                                  start=True, stop=True),
                     reads=[R("sq"), R("ones")], writes=[R("pB")])
                P.op("scalar", lambda E: E.activation(out=row_a[:, :TL], in_=pB[0:1, :TL],
                                                      func=AF.Ln, bias=biasc[0:1, 0:1]),
                     reads=[R("pB"), R("biasc")], writes=[R("rowa")])
                P.op("scalar", lambda E: E.activation(out=row_b[:, :TL], in_=row_a[:, :TL],
                                                      func=AF.Exp, scale=-0.5),
                     reads=[R("rowa")], writes=[R("rowb")])
                P.op("vector", lambda E, brow=brow: E.tensor_mul(row_c[:, :TL], row_b[:, :TL],
                                                                 brow[:, :TL]),
                     reads=[R("rowb"), R(bres)], writes=[R("rowc")])
                P.op("tensor", lambda E: E.matmul(pR[:, :TL], onesr_sb[:], row_c[:, :TL],
                                                  start=True, stop=True),
                     reads=[R("rowc"), R("onesr")], writes=[R("pR")])
                for b in range(TL // 128):
                    c0 = cc * SEG + tok0 + b * 128
                    ci = cc * CPS + (tok0 + b * 128) // 128
                    P.op("vector",
                         lambda E, c0=c0, ci=ci: E.tensor_copy(lastc[:, ci:ci + 1], cumc[:, c0 + 127:c0 + 128]),
                         reads=[R("cumc")], writes=[R("lastc")])
                    P.op("vector",
                         lambda E, ci=ci: E.tensor_scalar(refc[:, ci:ci + 1], lastc[:, ci:ci + 1],
                                                          0.5, None, ALU.mult),
                         reads=[R("lastc")], writes=[R("refc")])
                    P.op("scalar",
                         lambda E, ci=ci: E.activation(out=eref[:, ci:ci + 1], in_=refc[:, ci:ci + 1],
                                                       func=AF.Exp),
                         reads=[R("refc")], writes=[R("eref")])
                    P.op("scalar",
                         lambda E, ci=ci: E.activation(out=glast[:, ci:ci + 1], in_=lastc[:, ci:ci + 1],
                                                       func=AF.Exp),
                         reads=[R("lastc")], writes=[R("glast")])
                    P.op("scalar",
                         lambda E, ci=ci: E.activation(out=elmr[:, ci:ci + 1], in_=lastc[:, ci:ci + 1],
                                                       func=AF.Exp, scale=0.5),
                         reads=[R("lastc")], writes=[R("elmr")])
                    P.op("vector",
                         lambda E, c0=c0, ci=ci: E.tensor_scalar(cumc[:, c0:c0 + 128], cumc[:, c0:c0 + 128],
                                                                 refc[:, ci:ci + 1], None, ALU.subtract),
                         reads=[R("refc")], writes=[R("cumc")])
                    P.op("tensor",
                         lambda E, c0=c0: E.transpose(pTf[:], cumc[:, c0:c0 + 128], id_f32_sb[:]),
                         reads=[R("cumc"), R("idf")], writes=[R("pTf")])
                    P.op("scalar",
                         lambda E, c0=c0: E.activation(out=gres[:, c0:c0 + 128], in_=pTf[:],
                                                       func=AF.Exp),
                         reads=[R("pTf")], writes=[R("gres")])
                    P.op("scalar",
                         lambda E, c0=c0, b=b: E.activation(out=g_ln[:, b * 128:(b + 1) * 128],
                                                            in_=cumc[:, c0:c0 + 128], func=AF.Exp, scale=-1.0),
                         reads=[R("cumc")], writes=[R("gln")])
                    P.op("vector",
                         lambda E, wk0=wk0, b=b: E.tensor_mul(g_min[:, b * 128:(b + 1) * 128],
                                                              wtmp_b[1][:, wk0 + b * 128:wk0 + (b + 1) * 128],
                                                              pR[:, b * 128:(b + 1) * 128]),
                         reads=[R("wtmp1"), R("pR")], writes=[R("gmin")])
                    P.op("vector",
                         lambda E, c0=c0, b=b: E.tensor_mul(uT[:, c0:c0 + 128],
                                                            g_min[:, b * 128:(b + 1) * 128],
                                                            g_ln[:, b * 128:(b + 1) * 128]),
                         reads=[R("gmin"), R("gln")], writes=[R("uT")])
                    bt, btn = bkT[cc], f"bkT{cc}"
                    P.op("tensor",
                         lambda E, c0=c0, bt=bt: E.transpose(bt[:, 0:128], uT[:, c0:c0 + 128], id_sb[:]),
                         reads=[R("uT"), R("id")], writes=[R(btn)])
                    P.op("vector",
                         lambda E, c0=c0, bt=bt: E.tensor_copy(u_tm[:, c0:c0 + 128], bt[:, 0:128]),
                         reads=[R(btn)], writes=[R("u_tm")])
                    P.op("tensor",
                         lambda E, c0=c0, bt=bt: E.transpose(bt[:, 128:256], kT[:, c0:c0 + 128], id_sb[:]),
                         reads=[R("kT"), R("id")], writes=[R(btn)])
                    P.op("vector",
                         lambda E, c0=c0, bt=bt: E.tensor_copy(k_tm[:, c0:c0 + 128], bt[:, 128:256]),
                         reads=[R(btn)], writes=[R("k_tm")])

        def scan_chunk(s, h, b):
            c0 = h * SEG + b * 128
            ci = h * CPS + b
            glob_chunk = s * CPS + b
            pp = glob_chunk % 2
            nsk, nsv = f"SkT{h}", f"SvT{h}"
            ndm, nsc = f"Skdm{h}", f"Svsc{h}"
            bA, nA = bkA[h], f"bkA{h}"
            bB, nB = bkB[h], f"bkB{h}"
            bT, nT = bkT[h], f"bkT{h}"
            gsl = gres[:, c0:c0 + 128]
            P.op("tensor", lambda E: E.matmul(bA[:, 0:128], kT[:, c0:c0 + 128], qT[:, c0:c0 + 128],
                                              start=True, stop=True),
                 reads=[R("kT"), R("qT")], writes=[R(nA)])
            yield
            P.op("vector", lambda E: E.scalar_tensor_tensor(atm[h][:], bA[:, 0:128], 1.0, triu_sb[:],
                                                            ALU.mult, ALU.mult),
                 reads=[R(nA), R("triu")], writes=[R(f"atm{h}")])
            yield
            P.op("tensor", lambda E: E.matmul(bB[:, 0:128], qT[:, c0:c0 + 128], Sk_dm[h][:],
                                              start=True, stop=False),
                 reads=[R("qT"), R(ndm)], writes=[R(nB)])
            yield
            P.op("tensor", lambda E: E.matmul(bB[:, 0:128], atm[h][:], u_tm[:, c0:c0 + 128],
                                              start=False, stop=True),
                 reads=[R(f"atm{h}"), R("u_tm")], writes=[R(nB)])
            yield
            P.op("vector", lambda E: E.scalar_tensor_tensor(s_sb[h][:], bB[:, 0:128], SCALE, gsl,
                                                            ALU.mult, ALU.mult),
                 reads=[R(nB), R("gres")], writes=[R(f"s{h}")])
            yield
            P.op("vector", lambda E: E.reduce_max(nmax[h][:], s_sb[h][:], axis=AX, negate=True),
                 reads=[R(f"s{h}")], writes=[R(f"nmax{h}")])
            yield
            P.op("scalar", lambda E: E.activation(out=e_sb[h][:], in_=s_sb[h][:], func=AF.Exp,
                                                  bias=nmax[h][:, 0:1], accum_out=rsum[h][:]),
                 reads=[R(f"s{h}"), R(f"nmax{h}")], writes=[R(f"e{h}"), R(f"rsum{h}")])
            yield
            P.op("scalar", lambda E: E.activation(out=rinv[h][:], in_=rsum[h][:], func=AF.Ln),
                 reads=[R(f"rsum{h}")], writes=[R(f"rinv{h}")])
            yield
            P.op("scalar", lambda E: E.activation(out=rinv[h][:], in_=rinv[h][:], func=AF.Exp,
                                                  scale=-1.0),
                 reads=[R(f"rinv{h}")], writes=[R(f"rinv{h}")])
            yield
            P.op("vector", lambda E: E.scalar_tensor_tensor(at_c[h][:], e_sb[h][:], rinv[h][:, 0:1],
                                                            gsl, ALU.mult, ALU.mult),
                 reads=[R(f"e{h}"), R(f"rinv{h}"), R("gres")], writes=[R(f"atc{h}")])
            yield
            P.op("tensor", lambda E: E.transpose(bT[:, 0:128], at_c[h][:], id_sb[:]),
                 reads=[R(f"atc{h}"), R("id")], writes=[R(nT)])
            yield
            P.op("vector", lambda E: E.tensor_copy(atT[h][:], bT[:, 0:128]),
                 reads=[R(nT)], writes=[R(f"atT{h}")])
            yield
            P.op("tensor", lambda E: E.matmul(bA[0:64, 128:192], uT[:, c0:c0 + 64], atT[h][:, 0:64],
                                              start=True, stop=True),
                 reads=[R("uT"), R(f"atT{h}")], writes=[R(nA)])
            yield
            P.op("tensor", lambda E: E.matmul(bA[64:128, 192:256], uT[:, c0 + 64:c0 + 128],
                                              atT[h][:, 64:128], start=True, stop=True),
                 reads=[R("uT"), R(f"atT{h}")], writes=[R(nA)])
            yield
            P.op("tensor", lambda E: E.matmul(bA[0:64, 192:256], uT[:, c0:c0 + 64], atT[h][:, 64:128],
                                              start=True, stop=True),
                 reads=[R("uT"), R(f"atT{h}")], writes=[R(nA)])
            yield
            P.op("vector", lambda E: E.tensor_mul(btm[h][0:64, 0:64], bA[0:64, 128:192],
                                                  triu_sb[0:64, 0:64]),
                 reads=[R(nA), R("triu")], writes=[R(f"btm{h}")])
            yield
            P.op("vector", lambda E: E.tensor_mul(btm[h][64:128, 64:128], bA[64:128, 192:256],
                                                  triu_sb[0:64, 0:64]),
                 reads=[R(nA), R("triu")], writes=[R(f"btm{h}")])
            yield
            P.op("vector", lambda E: E.tensor_copy(btm[h][0:64, 64:128], bA[0:64, 192:256]),
                 reads=[R(nA)], writes=[R(f"btm{h}")])
            yield
            P.op("tensor", lambda E: E.matmul(bB[:, 128:256], SvT_sc[h][:], atT[h][:],
                                              start=True, stop=False),
                 reads=[R(nsc), R(f"atT{h}")], writes=[R(nB)])
            yield
            P.op("tensor", lambda E: E.matmul(bB[:, 128:256], v_tm[:, c0:c0 + 128], btm[h][:],
                                              start=False, stop=True),
                 reads=[R("v_tm"), R(f"btm{h}")], writes=[R(nB)])
            yield
            P.op("scalar", lambda E: E.activation(out=osq[h][:], in_=bB[:, 128:256], func=AF.Square),
                 reads=[R(nB)], writes=[R(f"osq{h}")])
            yield
            P.op("tensor", lambda E: E.matmul(bB[0:1, 384:512], ones_sb[:], osq[h][:],
                                              start=True, stop=True),
                 reads=[R(f"osq{h}"), R("ones")], writes=[R(nB)])
            yield
            P.op("scalar", lambda E: E.activation(out=orow[h][:], in_=bB[0:1, 384:512], func=AF.Ln,
                                                  scale=1.0 / 128.0, bias=biasc[0:1, 1:2]),
                 reads=[R(nB), R("biasc")], writes=[R(f"orow{h}")])
            yield
            P.op("scalar", lambda E: E.activation(out=orow2[h][:], in_=orow[h][:], func=AF.Exp,
                                                  scale=-0.5),
                 reads=[R(f"orow{h}")], writes=[R(f"orow2{h}")])
            yield
            P.op("vector", lambda E: E.tensor_copy(orow_bf[h][:], orow2[h][:]),
                 reads=[R(f"orow2{h}")], writes=[R(f"orowbf{h}")])
            yield
            P.op("tensor", lambda E: E.matmul(bB[:, 256:384], onesr_sb[:], orow_bf[h][:],
                                              start=True, stop=True),
                 reads=[R(f"orowbf{h}"), R("onesr")], writes=[R(nB)])
            yield
            P.op("vector", lambda E: E.tensor_mul(o1[h][:], bB[:, 128:256], Gt[:, c0:c0 + 128]),
                 reads=[R(nB), R("Gt")], writes=[R(f"o1{h}")])
            yield
            P.op("vector", lambda E: E.tensor_mul(o_seg[:, c0:c0 + 128], o1[h][:], bB[:, 256:384]),
                 reads=[R(f"o1{h}"), R(nB)], writes=[R("o_seg")])
            yield
            P.op("tensor", lambda E: E.matmul(bA[:, 256:384], u_tm[:, c0:c0 + 128],
                                              k_tm[:, c0:c0 + 128], start=True, stop=True),
                 reads=[R("u_tm"), R("k_tm")], writes=[R(nA)])
            yield
            P.op("tensor", lambda E: E.matmul(bA[:, 384:512], u_tm[:, c0:c0 + 128],
                                              v_tm[:, c0:c0 + 128], start=True, stop=True),
                 reads=[R("u_tm"), R("v_tm")], writes=[R(nA)])
            yield
            src, dst = pp, 1 - pp
            P.op("vector", lambda E: E.tensor_scalar(tsk[h][:], bA[:, 256:384], elmr[:, ci:ci + 1],
                                                     None, ALU.mult),
                 reads=[R(nA), R("elmr")], writes=[R(f"tsk{h}")])
            yield
            P.op("vector", lambda E, src=src, dst=dst: E.scalar_tensor_tensor(
                SkT[h][:, dst * 128:dst * 128 + 128], SkT[h][:, src * 128:src * 128 + 128],
                glast[:, ci:ci + 1], tsk[h][:], ALU.mult, ALU.add),
                 reads=[R("glast"), R(f"tsk{h}")], writes=[R(nsk)])
            yield
            P.op("vector", lambda E: E.tensor_scalar(tsv[h][:], bA[:, 384:512], elmr[:, ci:ci + 1],
                                                     None, ALU.mult),
                 reads=[R(nA), R("elmr")], writes=[R(f"tsv{h}")])
            yield
            P.op("vector", lambda E, src=src, dst=dst: E.scalar_tensor_tensor(
                SvT[h][:, dst * 128:dst * 128 + 128], SvT[h][:, src * 128:src * 128 + 128],
                glast[:, ci:ci + 1], tsv[h][:], ALU.mult, ALU.add),
                 reads=[R("glast"), R(f"tsv{h}")], writes=[R(nsv)])
            yield
            if b + 1 < CPS:
                cin = h * CPS + b + 1
                P.op("vector", lambda E, dst=dst, cin=cin: E.tensor_scalar(
                    skstage[h][:], SkT[h][:, dst * 128:dst * 128 + 128], eref[:, cin:cin + 1],
                    None, ALU.mult),
                     reads=[R(nsk), R("eref")], writes=[R(f"skst{h}")])
                yield
                P.op("tensor", lambda E: E.transpose(bT[:, 128:256], skstage[h][:], id_sb[:]),
                     reads=[R(f"skst{h}"), R("id")], writes=[R(nT)])
                yield
                P.op("vector", lambda E: E.tensor_copy(Sk_dm[h][:], bT[:, 128:256]),
                     reads=[R(nT)], writes=[R(ndm)])
                yield
                P.op("vector", lambda E, dst=dst, cin=cin: E.tensor_scalar(
                    SvT_sc[h][:], SvT[h][:, dst * 128:dst * 128 + 128], eref[:, cin:cin + 1],
                    None, ALU.mult),
                     reads=[R(nsv), R("eref")], writes=[R(nsc)])
                yield

        def rescale_state(s):
            for h in range(2):
                pp = (s * CPS) % 2
                ci = h * CPS + 0
                nsk, nsv = f"SkT{h}", f"SvT{h}"
                ndm, nsc = f"Skdm{h}", f"Svsc{h}"
                bT, nT = bkT[h], f"bkT{h}"
                P.op("vector", lambda E, h=h, pp=pp, ci=ci: E.tensor_scalar(
                    skstage[h][:], SkT[h][:, pp * 128:pp * 128 + 128], eref[:, ci:ci + 1],
                    None, ALU.mult),
                     reads=[R(nsk), R("eref")], writes=[R(f"skst{h}")])
                P.op("tensor", lambda E, h=h, bT=bT: E.transpose(bT[:, 128:256], skstage[h][:],
                                                                 id_sb[:]),
                     reads=[R(f"skst{h}"), R("id")], writes=[R(nT)])
                P.op("vector", lambda E, h=h, bT=bT: E.tensor_copy(Sk_dm[h][:], bT[:, 128:256]),
                     reads=[R(nT)], writes=[R(ndm)])
                P.op("vector", lambda E, h=h, pp=pp, ci=ci: E.tensor_scalar(
                    SvT_sc[h][:], SvT[h][:, pp * 128:pp * 128 + 128], eref[:, ci:ci + 1],
                    None, ALU.mult),
                     reads=[R(nsv), R("eref")], writes=[R(nsc)])

        id_f32 = np.eye(128, dtype=np.float32)
        idf_h = nc.inline_tensor(id_f32, name="a_identf")
        id_f32_sb = sb("id_f32_sb", [128, 128], F32)
        rx["idf"] = Res()
        P.dma("gpsimd", id_f32_sb[:], idf_h.ap()[:, :], writes=[R("idf")])
        skstage = [sb(f"skstage{h}", [128, 128], BF16) for h in range(2)]
        for _h in range(2):
            rx[f"skst{_h}"] = Res()
        biasc = sb("biasc", [128, 4], F32)
        rx["biasc"] = Res()
        P.op("vector", lambda E: E.memset(biasc[:, 0:1], 1e-6), writes=[R("biasc")])
        P.op("vector", lambda E: E.memset(biasc[:, 1:2], 1e-5), writes=[R("biasc")])
        P.op("vector", lambda E: E.memset(biasc[:, 2:3], 1.0), writes=[R("biasc")])

        seg_hook = (ctx or {}).get("seg_hook")
        q_tick = (ctx or {}).get("q_tick")
        for s in range(NSEG):
            for k in range(16):
                P.op("sync",
                     lambda E, s=s, k=k: E.dma_start_transpose(
                         xt[:, k * SEG:(k + 1) * SEG],
                         xbf[s * SEG:(s + 1) * SEG, k * 128:(k + 1) * 128]),
                     reads=[R("xbf")], writes=[R("xt")],
                     extra=[xbf_tok[s]], dma=True)
            if DBG_PROJ:
                for t in range(SEG // 512):
                    proj_tile(s, t, t * 512, 512)
            if DBG_SCAN:
                rescale_state(s)
                for b in range(CPS):
                    gens = [scan_chunk(s, 0, b), scan_chunk(s, 1, b)]
                    alive = [True, True]
                    rnd = 0
                    while alive[0] or alive[1]:
                        for i in (0, 1):
                            if alive[i]:
                                try:
                                    next(gens[i])
                                except StopIteration:
                                    alive[i] = False
                        rnd += 1
                        if q_tick is not None and rnd % 7 == 0:
                            q_tick()
            for cc in range(2):
                tok = P.dma("sync", o_dst(s, cc),
                            o_seg[:, cc * SEG:(cc + 1) * SEG], reads=[R("o_seg")])
                out_toks.append(tok)

        if ctx is None:
            P.op("sync", lambda E: E.nop(), extra=out_toks)
            P.finalize(st)
        else:
            if seg_hook is not None:
                seg_hook(NSEG)
            ctx["a_out_toks"] = out_toks
            P.barrier()
    return nc



def build_Q(ctx=None):
    g = _q_gen(ctx)
    nc = next(g)
    if ctx is None or not ctx.get("q_defer"):
        for _ in g:
            pass
    else:
        ctx["q_steps"] = g
    return nc


def _q_gen(ctx=None):
    if ctx is None:
        nc = bass.Bass("TRN2", target_bir_lowering=False)
        wgu_q = nc.dram_tensor("wgu_q", [256, 16384], BF16, kind="ExternalOutput").ap()
        wpr_q = nc.dram_tensor("wpr_q", [1024, 2048], BF16, kind="ExternalOutput").ap()
        ngu, npr = 256, 1024
    else:
        nc = ctx["nc"]
        wgu_q = ctx["wgu_q"]
        wpr_q = ctx["wpr_q"]
        ngu, npr = ctx.get("q_rows", (256, 1024))
    wgu_sh = nc.dram_tensor("wgu_sh", [ngu, 16384], F32, kind="ExternalInput").ap()
    wpr_sh = nc.dram_tensor("wpr_sh", [npr, 2048], F32, kind="ExternalInput").ap()
    ones_h = nc.inline_tensor(np.ones((128, 1), dtype=bfnp), name="q_ones_col")
    sel0 = np.zeros((1, 128), dtype=bfnp); sel0[0, :64] = 1
    sel1 = np.zeros((1, 128), dtype=bfnp); sel1[0, 64:] = 1
    sel0_h = nc.inline_tensor(sel0, name="q_sel0")
    sel1_h = nc.inline_tensor(sel1, name="q_sel1")
    sel0f_h = nc.inline_tensor(sel0.astype(np.float32), name="q_sel0f")
    sel1f_h = nc.inline_tensor(sel1.astype(np.float32), name="q_sel1f")

    with contextlib.ExitStack() as st:
        sb = lambda name, shape, dt: st.enter_context(nc.sbuf_tensor("q_" + name, shape, dt))
        w_bf = sb("w_bf", [128, 4096], BF16)
        q_bf = sb("q_bf", [128, 4096], BF16)
        aw = [sb(f"aw{i}", [128, 512], BF16) for i in range(2)]
        r_bf = [sb(f"r_bf{i}", [128, 512], BF16) for i in range(2)]
        mpos = [sb(f"mpos{i}", [128, 512], BF16) for i in range(2)]
        mneg = [sb(f"mneg{i}", [128, 512], BF16) for i in range(2)]
        s0r = [sb(f"s0r{i}", [1, 512], BF16) for i in range(2)]
        s1r = [sb(f"s1r{i}", [1, 512], BF16) for i in range(2)]
        si0 = [sb(f"si0{i}", [1, 512], F32) for i in range(2)]
        si1 = [sb(f"si1{i}", [1, 512], F32) for i in range(2)]
        ones_sb = sb("ones_sb", [128, 1], BF16)
        sel0_sb = sb("sel0_sb", [1, 128], BF16)
        sel1_sb = sb("sel1_sb", [1, 128], BF16)
        sel0f_sb = sb("sel0f_sb", [1, 128], F32)
        sel1f_sb = sb("sel1f_sb", [1, 128], F32)
        pSS = st.enter_context(nc.psum_tensor("q_pSS", [128, 512], F32))
        prr = st.enter_context(nc.psum_tensor("q_prr", [1, 512], F32))

        P = Prog(nc) if ctx is None else ctx["P"]
        rx = {}
        for n in ("wbf", "qbf", "ones", "sel0", "sel1", "sel0f", "sel1f"):
            rx[n] = Res()
        for i in range(2):
            for n in ("aw", "r", "mp", "mn", "s0", "s1", "si0", "si1"):
                rx[f"{n}{i}"] = Res()
        rx["pSS"] = Res(excl=True)
        rx["prr"] = Res(excl=True)
        R = lambda n: rx[n]
        P.dma("gpsimd", ones_sb[:], ones_h.ap()[:, :], writes=[R("ones")])
        P.dma("gpsimd", sel0_sb[:], sel0_h.ap()[:, :], writes=[R("sel0")])
        P.dma("gpsimd", sel1_sb[:], sel1_h.ap()[:, :], writes=[R("sel1")])
        P.dma("gpsimd", sel0f_sb[:], sel0f_h.ap()[:, :], writes=[R("sel0f")])
        P.dma("gpsimd", sel1f_sb[:], sel1f_h.ap()[:, :], writes=[R("sel1f")])
        out_toks = []

        def quant_rows(src, dst, nrow, ncol):
            npiece = max(1, ncol // 4096)
            pw = ncol // npiece
            for pc0 in range(nrow // 128 * npiece):
                pc, pcc = divmod(pc0, npiece)
                cs = slice(pcc * pw, (pcc + 1) * pw)
                P.dma("gpsimd", w_bf[:, :pw], src[pc * 128:(pc + 1) * 128, cs],
                      writes=[R("wbf")])
                for isub, f0 in enumerate(range(0, pw, 512)):
                    fs = slice(f0, f0 + 512)
                    i = isub % 2
                    awi, ri, mpi, mni = aw[i], r_bf[i], mpos[i], mneg[i]
                    s0i, s1i, si0i, si1i = s0r[i], s1r[i], si0[i], si1[i]
                    sfx = str(i)
                    P.op("scalar", lambda E, fs=fs, awi=awi: E.activation(
                        out=awi[:], in_=w_bf[:, fs], func=AF.Abs),
                         reads=[R("wbf")], writes=[R("aw" + sfx)])
                    P.op("tensor", lambda E, awi=awi: E.matmul(
                        prr[:], ones_sb[0:64, :], awi[0:64, :], start=True, stop=True),
                         reads=[R("aw" + sfx), R("ones")], writes=[R("prr")])
                    P.op("vector", lambda E, s0i=s0i: E.tensor_scalar(
                        s0i[:], prr[:], 1.0 / 64.0, None, ALU.mult),
                         reads=[R("prr")], writes=[R("s0" + sfx)])
                    P.op("tensor", lambda E, awi=awi: E.matmul(
                        prr[:], ones_sb[64:128, :], awi[64:128, :], start=True, stop=True),
                         reads=[R("aw" + sfx), R("ones")], writes=[R("prr")])
                    P.op("vector", lambda E, s1i=s1i: E.tensor_scalar(
                        s1i[:], prr[:], 1.0 / 64.0, None, ALU.mult),
                         reads=[R("prr")], writes=[R("s1" + sfx)])
                    P.op("vector", lambda E, si0i=si0i, s0i=s0i: E.reciprocal(si0i[:], s0i[:]),
                         reads=[R("s0" + sfx)], writes=[R("si0" + sfx)])
                    P.op("vector", lambda E, si1i=si1i, s1i=s1i: E.reciprocal(si1i[:], s1i[:]),
                         reads=[R("s1" + sfx)], writes=[R("si1" + sfx)])
                    P.op("tensor", lambda E, si0i=si0i: E.matmul(
                        pSS[:], sel0f_sb[:], si0i[:], start=True, stop=False),
                         reads=[R("si0" + sfx), R("sel0f")], writes=[R("pSS")])
                    P.op("tensor", lambda E, si1i=si1i: E.matmul(
                        pSS[:], sel1f_sb[:], si1i[:], start=False, stop=True),
                         reads=[R("si1" + sfx), R("sel1f")], writes=[R("pSS")])
                    P.op("vector", lambda E, fs=fs, ri=ri: E.tensor_mul(
                        ri[:], w_bf[:, fs], pSS[:]),
                         reads=[R("wbf"), R("pSS")], writes=[R("r" + sfx)])
                    P.op("tensor", lambda E, s0i=s0i: E.matmul(
                        pSS[:], sel0_sb[:], s0i[:], start=True, stop=False),
                         reads=[R("s0" + sfx), R("sel0")], writes=[R("pSS")])
                    P.op("tensor", lambda E, s1i=s1i: E.matmul(
                        pSS[:], sel1_sb[:], s1i[:], start=False, stop=True),
                         reads=[R("s1" + sfx), R("sel1")], writes=[R("pSS")])
                    P.op("vector", lambda E, mpi=mpi, ri=ri: E.tensor_single_scalar(
                        mpi[:], ri[:], 0.5, ALU.is_gt),
                         reads=[R("r" + sfx)], writes=[R("mp" + sfx)])
                    P.op("vector", lambda E, mni=mni, ri=ri: E.tensor_single_scalar(
                        mni[:], ri[:], -0.5, ALU.is_lt),
                         reads=[R("r" + sfx)], writes=[R("mn" + sfx)])
                    P.op("vector", lambda E, mpi=mpi, mni=mni: E.tensor_sub(
                        mpi[:], mpi[:], mni[:]),
                         reads=[R("mn" + sfx)], writes=[R("mp" + sfx)])
                    P.op("vector", lambda E, fs=fs, mpi=mpi: E.tensor_mul(
                        q_bf[:, fs], mpi[:], pSS[:]),
                         reads=[R("mp" + sfx), R("pSS")], writes=[R("qbf")])
                    yield
                tok = P.dma("sync", dst[pc * 128:(pc + 1) * 128, cs], q_bf[:, :pw],
                            reads=[R("qbf")])
                out_toks.append(tok)
                yield

        yield nc
        yield from quant_rows(wgu_sh, wgu_q, ngu, 16384)
        yield from quant_rows(wpr_sh, wpr_q, npr, 2048)
        if ctx is None:
            P.op("sync", lambda E: E.nop(), extra=out_toks)
            P.finalize(st)
        else:
            ctx["q_out_toks"] = out_toks
            yield "done"


LNEPS = 1e-5
RMSEPS = float(np.finfo(np.float32).eps)


def build_B(ctx=None):
    if ctx is None:
        nc = bass.Bass("TRN2", target_bir_lowering=False)
        oT_src = nc.dram_tensor("oT", [128, 16 * 1024], BF16, kind="ExternalInput").ap()[:, :]
        wguq_Pt = nc.dram_tensor("wguq_P", [128, 128, 2048], BF16, kind="ExternalInput").ap()
        wprq_Pt = nc.dram_tensor("wprq_P", [16, 128, 8192], BF16, kind="ExternalInput").ap()
        gu_src = lambda ocb: wguq_Pt[ocb, :, :]
        pr_src = lambda oc, half: wprq_Pt[oc, :, half * 4096:(half + 1) * 4096]
        cgate = []
        ctx = None
    else:
        nc = ctx["nc"]
        oT_src = ctx["oT_src"]
        gu_src = ctx["gu_src"]
        pr_src = ctx["pr_src"]
        cgate = ctx["cgate"]
    pre_oT = ctx.get("pre_oT") if ctx else None
    pre_gu = ctx.get("pre_gu") if ctx else None
    x_sl = nc.dram_tensor("x_sl", [1024, HID], F32, kind="ExternalInput").ap()
    wo_r = nc.dram_tensor("wo_r", [16, 128, 2048], F32, kind="ExternalInput").ap()
    rg_r = nc.dram_tensor("rg_r", [128, 128], F32, kind="ExternalInput").ap()
    egu_g = nc.dram_tensor("egu_g", [8, 8, 128, 2048], F32, kind="ExternalInput").ap()
    egu_u = nc.dram_tensor("egu_u", [8, 8, 128, 2048], F32, kind="ExternalInput").ap()
    edn_r = nc.dram_tensor("edn_r", [16, 128, 8192], F32, kind="ExternalInput").ap()
    bt_r = nc.dram_tensor("bt_r", [4, 128, 4096], F32, kind="ExternalInput").ap()
    ct_r = nc.dram_tensor("ct_r", [4, 128, 4096], F32, kind="ExternalInput").ap()
    dvec = nc.dram_tensor("dvec", [128, 16], F32, kind="ExternalInput").ap()
    out = nc.dram_tensor("out", [1024, HID], F32, kind="ExternalOutput").ap()

    identf_h = nc.inline_tensor(np.eye(128, dtype=np.float32), name="b_identf")
    id8f_h = nc.inline_tensor(np.eye(8, dtype=np.float32), name="b_id8f")
    ones_h = nc.inline_tensor(np.ones((128, 1), dtype=bfnp), name="b_ones_col")
    onesrf_h = nc.inline_tensor(np.ones((1, 128), dtype=np.float32), name="b_ones_rowf")
    sel = np.zeros((8, 8 * 128), dtype=bfnp)
    for e in range(8):
        sel[e, e * 128:(e + 1) * 128] = 1
    sel_h = nc.inline_tensor(sel, name="b_sel_e")

    with contextlib.ExitStack() as st:
        sb = lambda name, shape, dt: st.enter_context(nc.sbuf_tensor("b_" + name, shape, dt))
        resid = sb("resid", [128, 16 * 1024], F32)
        xbf_s = sb("xbf_s", [128, 16 * 1024], BF16)
        hbuf = sb("hbuf", [128, 64 * 512], BF16)
        wsA = sb("wsA", [128, 4096], BF16)
        wsB = sb("wsB", [128, 4096], BF16)
        stb = hbuf
        idf_sb = sb("idf_sb", [128, 128], F32)
        id8f_sb = sb("id8f_sb", [8, 8], F32)
        ones_sb = sb("ones_sb", [128, 1], BF16)
        onesrf_sb = sb("onesrf_sb", [1, 128], F32)
        sel_sb = sb("sel_sb", [8, 8 * 128], BF16)
        rg_sb = sb("rg_sb", [128, 128], BF16)
        dv_sb = sb("dv_sb", [128, 16], F32)
        biasc = sb("biasc", [128, 4], F32)
        sq_bf = sb("sq_bf", [128, 512], BF16)
        sq_bf2 = sb("sq_bf2", [128, 512], BF16)
        xtile = sb("xtile", [128, 512], F32)
        lgf = sb("lgf", [8, 512], F32)
        lg_tm = sb("lg_tm", [128, 32], F32)
        m1c = sb("m1c", [128, 1], F32)
        m2c = sb("m2c", [128, 1], F32)
        eq1 = sb("eq1", [128, 8], F32)
        eq2 = sb("eq2", [128, 8], F32)
        l2m = sb("l2m", [128, 8], F32)
        d12 = sb("d12", [128, 1], F32)
        s12 = sb("s12", [128, 1], F32)
        s21 = sb("s21", [128, 1], F32)
        we_tm = sb("we_tm", [128, 32], F32)
        weT = sb("weT", [8, 2 * 512], BF16)
        webc_sb = sb("webc_sb", [128, 512], BF16)
        sgt = sb("sgt", [128, 512], BF16)
        h1t = sb("h1t", [128, 512], BF16)
        rowa = sb("rowa", [1, 512], F32)
        rowb = sb("rowb", [1, 512], F32)
        rowc = sb("rowc", [1, 512], F32)
        rowd = sb("rowd", [1, 512], F32)
        p0 = st.enter_context(nc.psum_tensor("b_p0", [128, 512], F32))
        p1 = st.enter_context(nc.psum_tensor("b_p1", [128, 512], F32))
        p2 = st.enter_context(nc.psum_tensor("b_p2", [128, 512], F32))
        p3 = st.enter_context(nc.psum_tensor("b_p3", [128, 512], F32))
        pTf = st.enter_context(nc.psum_tensor("b_pTf", [128, 128], F32))
        pS1 = st.enter_context(nc.psum_tensor("b_pS1", [1, 512], F32))
        pS2 = st.enter_context(nc.psum_tensor("b_pS2", [1, 512], F32))

        P = Prog(nc) if ctx is None else ctx["P"]
        rx = {n: Res() for n in (
            "resid", "xbf", "hbuf", "stb", "idf", "id8f", "ones", "onesrf",
            "sel", "rg", "dv", "biasc", "sq", "sq2", "xtile", "lgf", "lgtm", "m1",
            "m2", "eq1", "eq2", "l2m", "d12", "s12", "s21", "wetm", "weT",
            "sgt", "h1t", "rowa", "rowb", "rowc", "rowd", "webc")}
        for a in "AB":
            for q in range(2):
                rx[f"ws{a}{q}"] = Res()
        for n in ("p0", "p1", "p2", "p3", "pTf", "pS1", "pS2"):
            rx[n] = Res(excl=True)
        R = lambda n: rx[n]
        AR = {"A": wsA, "B": wsB}

        def qslice(a, q):
            return AR[a][:, q * 2048:(q + 1) * 2048]

        def load_quarter(a, q, src, eng="gpsimd"):
            return P.dma(eng, qslice(a, q), src, writes=[R(f"ws{a}{q}")])

        def load_arena(a, src, eng="gpsimd"):
            return P.dma(eng, AR[a][:, :], src,
                         writes=[R(f"ws{a}{q}") for q in range(2)])

        def arena_res(a):
            return [R(f"ws{a}{q}") for q in range(2)]

        def gu_load(a, q, ocb):
            if ctx is None:
                P.dma("sync", qslice(a, q), gu_src(ocb), writes=[R(f"ws{a}{q}")])
            else:
                dview = qslice(a, q).rearrange("p (k c) -> p k c", k=16, c=128)
                P.dma("sync", dview, gu_src(ocb), writes=[R(f"ws{a}{q}")],
                      extra=list(cgate))

        def pr_load(a, oc, half):
            if ctx is None:
                P.dma("sync", AR[a][:, :], pr_src(oc, half),
                      writes=[R(f"ws{a}0"), R(f"ws{a}1")])
            else:
                dview = AR[a][:, :].rearrange("p (hc c) -> p hc c", hc=32, c=128)
                P.dma("sync", dview, pr_src(oc, half),
                      writes=[R(f"ws{a}0"), R(f"ws{a}1")], extra=list(cgate))

        P.dma("gpsimd", idf_sb[:], identf_h.ap()[:, :], writes=[R("idf")])
        P.dma("gpsimd", id8f_sb[:], id8f_h.ap()[:, :], writes=[R("id8f")])
        P.dma("gpsimd", ones_sb[:], ones_h.ap()[:, :], writes=[R("ones")])
        P.dma("gpsimd", onesrf_sb[:], onesrf_h.ap()[:, :], writes=[R("onesrf")])
        P.dma("gpsimd", sel_sb[:], sel_h.ap()[:, :], writes=[R("sel")])
        P.dma("gpsimd", rg_sb[:], rg_r[:, :], writes=[R("rg")])
        P.dma("gpsimd", dv_sb[:], dvec[:, :], writes=[R("dv")])
        P.op("vector", lambda E: E.memset(biasc[:, 0:1], LNEPS), writes=[R("biasc")])
        P.op("vector", lambda E: E.memset(biasc[:, 1:2], RMSEPS), writes=[R("biasc")])

        for tt in range(8):
            for cs in range(4):
                P.dma("gpsimd", xtile[:],
                      x_sl[tt * 128:(tt + 1) * 128, cs * 512:(cs + 1) * 512],
                      writes=[R("xtile")])
                for k4 in range(4):
                    k = cs * 4 + k4
                    P.op("tensor", lambda E, k4=k4: E.transpose(
                        pTf[:], xtile[:, k4 * 128:(k4 + 1) * 128], idf_sb[:]),
                         reads=[R("xtile"), R("idf")], writes=[R("pTf")])
                    P.op("vector", lambda E, k=k, tt=tt: E.tensor_copy(
                        resid[:, k * 1024 + tt * 128:k * 1024 + tt * 128 + 128], pTf[:]),
                         reads=[R("pTf")], writes=[R("resid")])
        if pre_oT is not None:
            pre_oT(P)
        P.dma("gpsimd", hbuf[:, 0:16 * 1024], oT_src, writes=[R("hbuf")],
              extra=list(cgate))

        def ln_stage():
            for t in range(2):
                for k in range(16):
                    c0 = k * 1024 + t * 512
                    sq, sqn = (sq_bf, "sq") if k % 2 == 0 else (sq_bf2, "sq2")
                    P.op("vector", lambda E, c0=c0, sq=sq: E.tensor_copy(sq[:], resid[:, c0:c0 + 512]),
                         reads=[R("resid")], writes=[R(sqn)])
                    P.op("tensor", lambda E, k=k, sq=sq: E.matmul(pS1[:], ones_sb[:], sq[:],
                                                                  start=(k == 0), stop=(k == 15),
                                                                  skip_group_check=True),
                         reads=[R(sqn), R("ones")], writes=[R("pS1")])
                    P.op("scalar", lambda E, sq=sq: E.activation(out=sq[:], in_=sq[:], func=AF.Square),
                         reads=[R(sqn)], writes=[R(sqn)])
                    P.op("tensor", lambda E, k=k, sq=sq: E.matmul(pS2[:], ones_sb[:], sq[:],
                                                                  start=(k == 0), stop=(k == 15),
                                                                  skip_group_check=True),
                         reads=[R(sqn), R("ones")], writes=[R("pS2")])
                P.op("vector", lambda E: E.tensor_scalar(rowa[:], pS1[:], 1.0 / 2048.0, None, ALU.mult),
                     reads=[R("pS1")], writes=[R("rowa")])
                P.op("scalar", lambda E: E.activation(out=rowb[:], in_=rowa[:], func=AF.Square),
                     reads=[R("rowa")], writes=[R("rowb")])
                P.op("vector", lambda E: E.scalar_tensor_tensor(rowc[:], pS2[:], 1.0 / 2048.0,
                                                                rowb[:], ALU.mult, ALU.subtract),
                     reads=[R("pS2"), R("rowb")], writes=[R("rowc")])
                P.op("scalar", lambda E: E.activation(out=rowc[:], in_=rowc[:], func=AF.Ln,
                                                      bias=biasc[0:1, 0:1]),
                     reads=[R("biasc")], writes=[R("rowc")])
                P.op("scalar", lambda E: E.activation(out=rowd[:], in_=rowc[:], func=AF.Exp,
                                                      scale=-0.5),
                     reads=[R("rowc")], writes=[R("rowd")])
                P.op("vector", lambda E: E.tensor_mul(rowb[:], rowa[:], rowd[:]),
                     reads=[R("rowa"), R("rowd")], writes=[R("rowb")])
                P.op("vector", lambda E: E.tensor_scalar(rowb[:], rowb[:], -1.0, None, ALU.mult),
                     writes=[R("rowb")])
                P.op("tensor", lambda E: E.matmul(p2[:], onesrf_sb[:], rowd[:],
                                                  start=True, stop=True),
                     reads=[R("rowd"), R("onesrf")], writes=[R("p2")])
                P.op("tensor", lambda E: E.matmul(p3[:], onesrf_sb[:], rowb[:],
                                                  start=True, stop=True),
                     reads=[R("rowb"), R("onesrf")], writes=[R("p3")])
                for k in range(16):
                    c0 = k * 1024 + t * 512
                    P.op("vector", lambda E, c0=c0: E.tensor_mul(resid[:, c0:c0 + 512],
                                                                 resid[:, c0:c0 + 512], p2[:]),
                         reads=[R("p2")], writes=[R("resid")])
                    P.op("vector", lambda E, c0=c0: E.tensor_add(resid[:, c0:c0 + 512],
                                                                 resid[:, c0:c0 + 512], p3[:]),
                         reads=[R("p3")], writes=[R("resid")])
                    P.op("gpsimd", lambda E, c0=c0: E.tensor_copy(xbf_s[:, c0:c0 + 512],
                                                                  resid[:, c0:c0 + 512]),
                         reads=[R("resid")], writes=[R("xbf")])

        for oc in range(16):
            a, q = ("A", "B")[(oc // 2) % 2], oc % 2
            load_quarter(a, q, wo_r[oc, :, :])
            for t in range(2):
                pb, pn = (p0, "p0") if (oc * 2 + t) % 2 == 0 else (p1, "p1")
                for k in range(16):
                    P.op("tensor", lambda E, a=a, q=q, k=k, t=t, pb=pb: E.matmul(
                        pb[:], qslice(a, q)[:, k * 128:(k + 1) * 128],
                        hbuf[:, k * 1024 + t * 512:k * 1024 + t * 512 + 512],
                        start=(k == 0), stop=(k == 15)),
                         reads=[R(f"ws{a}{q}"), R("hbuf")], writes=[R(pn)])
                c0 = oc * 1024 + t * 512
                P.op("vector", lambda E, c0=c0, pb=pb: E.tensor_add(resid[:, c0:c0 + 512],
                                                                    resid[:, c0:c0 + 512], pb[:]),
                     reads=[R(pn)], writes=[R("resid")])
        ln_stage()

        for t in range(2):
            for k in range(16):
                P.op("tensor", lambda E, k=k, t=t: E.matmul(
                    p0[0:8, :], rg_sb[:, k * 8:(k + 1) * 8],
                    xbf_s[:, k * 1024 + t * 512:k * 1024 + t * 512 + 512],
                    start=(k == 0), stop=(k == 15)),
                     reads=[R("rg"), R("xbf")], writes=[R("p0")])
            P.op("vector", lambda E: E.tensor_copy(lgf[:], p0[0:8, :]),
                 reads=[R("p0")], writes=[R("lgf")])
            for b in range(4):
                P.op("tensor", lambda E, b=b: E.transpose(pTf[0:128, 0:8],
                                                          lgf[:, b * 128:(b + 1) * 128],
                                                          id8f_sb[:]),
                     reads=[R("lgf"), R("id8f")], writes=[R("pTf")])
                P.op("vector", lambda E, b=b: E.tensor_copy(lg_tm[:, b * 8:(b + 1) * 8],
                                                            pTf[0:128, 0:8]),
                     reads=[R("pTf")], writes=[R("lgtm")])
            for b in range(4):
                P.op("vector", lambda E, b=b: E.reduce_max(m1c[:], lg_tm[:, b * 8:(b + 1) * 8],
                                                           axis=AX),
                     reads=[R("lgtm")], writes=[R("m1")])
                P.op("vector", lambda E, b=b: E.tensor_scalar(eq1[:], lg_tm[:, b * 8:(b + 1) * 8],
                                                              m1c[:, 0:1], None, ALU.is_equal),
                     reads=[R("lgtm"), R("m1")], writes=[R("eq1")])
                P.op("vector", lambda E, b=b: E.scalar_tensor_tensor(
                    l2m[:], eq1[:], -1e9, lg_tm[:, b * 8:(b + 1) * 8], ALU.mult, ALU.add),
                     reads=[R("eq1"), R("lgtm")], writes=[R("l2m")])
                P.op("vector", lambda E: E.reduce_max(m2c[:], l2m[:], axis=AX),
                     reads=[R("l2m")], writes=[R("m2")])
                P.op("vector", lambda E: E.tensor_scalar(eq2[:], l2m[:], m2c[:, 0:1], None,
                                                         ALU.is_equal),
                     reads=[R("l2m"), R("m2")], writes=[R("eq2")])
                P.op("vector", lambda E: E.tensor_sub(d12[:], m1c[:], m2c[:]),
                     reads=[R("m1"), R("m2")], writes=[R("d12")])
                P.op("scalar", lambda E: E.activation(out=s12[:], in_=d12[:], func=AF.Sigmoid),
                     reads=[R("d12")], writes=[R("s12")])
                P.op("scalar", lambda E: E.activation(out=s21[:], in_=d12[:], func=AF.Sigmoid,
                                                      scale=-1.0),
                     reads=[R("d12")], writes=[R("s21")])
                P.op("vector", lambda E: E.tensor_scalar(eq1[:], eq1[:], s12[:, 0:1], None, ALU.mult),
                     reads=[R("s12")], writes=[R("eq1")])
                P.op("vector", lambda E, b=b: E.scalar_tensor_tensor(
                    we_tm[:, b * 8:(b + 1) * 8], eq2[:], s21[:, 0:1], eq1[:], ALU.mult, ALU.add),
                     reads=[R("eq2"), R("s21"), R("eq1")], writes=[R("wetm")])
            for b in range(4):
                P.op("tensor", lambda E, b=b: E.transpose(pTf[0:8, 0:128],
                                                          we_tm[:, b * 8:(b + 1) * 8], idf_sb[:]),
                     reads=[R("wetm"), R("idf")], writes=[R("pTf")])
                P.op("vector", lambda E, b=b, t=t: E.tensor_copy(
                    weT[0:8, t * 512 + b * 128:t * 512 + (b + 1) * 128], pTf[0:8, 0:128]),
                     reads=[R("pTf")], writes=[R("weT")])

        for t in range(2):
            for e in range(8):
                P.op("tensor", lambda E, e=e, t=t: E.matmul(
                    p2[:], sel_sb[:, e * 128:(e + 1) * 128], weT[:, t * 512:(t + 1) * 512],
                    start=True, stop=True),
                     reads=[R("sel"), R("weT")], writes=[R("p2")])
                P.op("vector", lambda E: E.tensor_copy(webc_sb[:], p2[:]),
                     reads=[R("p2")], writes=[R("webc")])
                for oc in range(8):
                    a = "A" if (e * 8 + oc) % 2 == 0 else "B"
                    it = e * 8 + oc
                    pg, pgn = (p0, "p0") if it % 2 == 0 else (p2, "p2")
                    pu, pun = (p1, "p1") if it % 2 == 0 else (p3, "p3")
                    load_quarter(a, 0, egu_g[e, oc, :, :])
                    load_quarter(a, 1, egu_u[e, oc, :, :])
                    for k in range(16):
                        P.op("tensor", lambda E, a=a, k=k, t=t, pg=pg: E.matmul(
                            pg[:], qslice(a, 0)[:, k * 128:(k + 1) * 128],
                            xbf_s[:, k * 1024 + t * 512:k * 1024 + t * 512 + 512],
                            start=(k == 0), stop=(k == 15)),
                             reads=[R(f"ws{a}0"), R("xbf")], writes=[R(pgn)])
                    for k in range(16):
                        P.op("tensor", lambda E, a=a, k=k, t=t, pu=pu: E.matmul(
                            pu[:], qslice(a, 1)[:, k * 128:(k + 1) * 128],
                            xbf_s[:, k * 1024 + t * 512:k * 1024 + t * 512 + 512],
                            start=(k == 0), stop=(k == 15)),
                             reads=[R(f"ws{a}1"), R("xbf")], writes=[R(pun)])
                    P.op("scalar", lambda E, pg=pg: E.activation(out=sgt[:], in_=pg[:], func=AF.Silu),
                         reads=[R(pgn)], writes=[R("sgt")])
                    P.op("vector", lambda E: E.tensor_mul(h1t[:], sgt[:], webc_sb[:]),
                         reads=[R("sgt"), R("webc")], writes=[R("h1t")])
                    hc0 = (e * 8 + oc) * 512
                    P.op("vector", lambda E, hc0=hc0, pu=pu: E.tensor_mul(hbuf[:, hc0:hc0 + 512],
                                                                          h1t[:], pu[:]),
                         reads=[R("h1t"), R(pun)], writes=[R("hbuf")])
            for oc in range(16):
                load_arena("A", edn_r[oc, :, 0:4096])
                load_arena("B", edn_r[oc, :, 4096:8192])
                pb, pn = (p0, "p0") if oc % 2 == 0 else (p1, "p1")
                for eh in range(64):
                    a = "A" if eh < 32 else "B"
                    eo = eh % 32
                    P.op("tensor", lambda E, a=a, eo=eo, eh=eh, pb=pb: E.matmul(
                        pb[:], AR[a][:, eo * 128:(eo + 1) * 128],
                        hbuf[:, eh * 512:eh * 512 + 512],
                        start=(eh == 0), stop=(eh == 63)),
                         reads=[R(f"ws{a}{eo // 16}"), R("hbuf")], writes=[R(pn)])
                c0 = oc * 1024 + t * 512
                P.op("vector", lambda E, c0=c0, pb=pb: E.tensor_add(resid[:, c0:c0 + 512],
                                                                    resid[:, c0:c0 + 512], pb[:]),
                     reads=[R(pn)], writes=[R("resid")])
        ln_stage()

        for quad in range(4):
            a = "A" if quad % 2 == 0 else "B"
            load_arena(a, bt_r[quad, :, :])
            for j in range(2):
                oc = quad * 2 + j
                for t in range(2):
                    pb, pn = (p0, "p0") if (oc * 2 + t) % 2 == 0 else (p1, "p1")
                    for k in range(16):
                        P.op("tensor", lambda E, a=a, j=j, k=k, t=t, pb=pb: E.matmul(
                            pb[:], AR[a][:, k * 256 + j * 128:k * 256 + (j + 1) * 128],
                            xbf_s[:, k * 1024 + t * 512:k * 1024 + t * 512 + 512],
                            start=(k == 0), stop=(k == 15)),
                             reads=arena_res(a) + [R("xbf")], writes=[R(pn)])
                    P.op("scalar", lambda E, oc=oc, t=t, pb=pb: E.activation(
                        out=stb[:, oc * 1024 + t * 512:oc * 1024 + t * 512 + 512],
                        in_=pb[:], func=AF.Tanh),
                         reads=[R(pn)], writes=[R("stb")])
        for quad in range(4):
            a = "A" if quad % 2 == 0 else "B"
            load_arena(a, ct_r[quad, :, :])
            for j in range(4):
                oc = quad * 4 + j
                for t in range(2):
                    pb, pn = (p0, "p0") if (oc * 2 + t) % 2 == 0 else (p1, "p1")
                    for k in range(8):
                        P.op("tensor", lambda E, a=a, j=j, k=k, t=t, pb=pb: E.matmul(
                            pb[:], AR[a][:, k * 512 + j * 128:k * 512 + (j + 1) * 128],
                            stb[:, k * 1024 + t * 512:k * 1024 + t * 512 + 512],
                            start=(k == 0), stop=(k == 7)),
                             reads=arena_res(a) + [R("stb")], writes=[R(pn)])
                    c0 = oc * 1024 + t * 512
                    P.op("vector", lambda E, c0=c0, oc=oc, pb=pb: E.scalar_tensor_tensor(
                        h1t[:], resid[:, c0:c0 + 512], dv_sb[:, oc:oc + 1], pb[:],
                        ALU.mult, ALU.add),
                         reads=[R("resid"), R("dv"), R(pn)], writes=[R("h1t")])
                    P.op("vector", lambda E, c0=c0: E.tensor_add(resid[:, c0:c0 + 512],
                                                                 resid[:, c0:c0 + 512], h1t[:]),
                         reads=[R("h1t")], writes=[R("resid")])
        ln_stage()

        if pre_gu is not None:
            pre_gu(P)
        for t in range(2):
            for ocb in range(64):
                a = "A" if ocb % 2 == 0 else "B"
                pg, pgn = (p0, "p0") if ocb % 2 == 0 else (p2, "p2")
                pu, pun = (p1, "p1") if ocb % 2 == 0 else (p3, "p3")
                sq, sqn = (sq_bf, "sq") if ocb % 2 == 0 else (sq_bf2, "sq2")
                gu_load(a, 0, ocb)
                gu_load(a, 1, ocb + 64)
                for k in range(16):
                    P.op("tensor", lambda E, a=a, k=k, t=t, pg=pg: E.matmul(
                        pg[:], qslice(a, 0)[:, k * 128:(k + 1) * 128],
                        xbf_s[:, k * 1024 + t * 512:k * 1024 + t * 512 + 512],
                        start=(k == 0), stop=(k == 15)),
                         reads=[R(f"ws{a}0"), R("xbf")], writes=[R(pgn)])
                for k in range(16):
                    P.op("tensor", lambda E, a=a, k=k, t=t, pu=pu: E.matmul(
                        pu[:], qslice(a, 1)[:, k * 128:(k + 1) * 128],
                        xbf_s[:, k * 1024 + t * 512:k * 1024 + t * 512 + 512],
                        start=(k == 0), stop=(k == 15)),
                         reads=[R(f"ws{a}1"), R("xbf")], writes=[R(pun)])
                P.op("scalar", lambda E, pg=pg: E.activation(out=sgt[:], in_=pg[:], func=AF.Silu),
                     reads=[R(pgn)], writes=[R("sgt")])
                hc0 = ocb * 512
                P.op("vector", lambda E, hc0=hc0, pu=pu: E.tensor_mul(hbuf[:, hc0:hc0 + 512],
                                                                      sgt[:], pu[:]),
                     reads=[R("sgt"), R(pun)], writes=[R("hbuf")])
                P.op("scalar", lambda E, hc0=hc0, sq=sq: E.activation(out=sq[:],
                                                                      in_=hbuf[:, hc0:hc0 + 512],
                                                                      func=AF.Square),
                     reads=[R("hbuf")], writes=[R(sqn)])
                P.op("tensor", lambda E, ocb=ocb, sq=sq: E.matmul(
                    pS1[:], ones_sb[:], sq[:],
                    start=(ocb == 0), stop=(ocb == 63), skip_group_check=True),
                     reads=[R(sqn), R("ones")], writes=[R("pS1")])
            P.op("scalar", lambda E: E.activation(out=rowc[:], in_=pS1[:], func=AF.Ln,
                                                  scale=1.0 / 8192.0, bias=biasc[0:1, 1:2]),
                 reads=[R("pS1"), R("biasc")], writes=[R("rowc")])
            P.op("scalar", lambda E: E.activation(out=rowd[:], in_=rowc[:], func=AF.Exp,
                                                  scale=-0.5),
                 reads=[R("rowc")], writes=[R("rowd")])
            P.op("tensor", lambda E: E.matmul(p2[:], onesrf_sb[:], rowd[:], start=True, stop=True),
                 reads=[R("rowd"), R("onesrf")], writes=[R("p2")])
            for hc in range(64):
                hc0 = hc * 512
                P.op("vector", lambda E, hc0=hc0: E.tensor_mul(hbuf[:, hc0:hc0 + 512],
                                                               hbuf[:, hc0:hc0 + 512], p2[:]),
                     reads=[R("p2")], writes=[R("hbuf")])
            for oc in range(16):
                pr_load("A", oc, 0)
                pr_load("B", oc, 1)
                pb, pn = (p0, "p0") if oc % 2 == 0 else (p1, "p1")
                for hc in range(64):
                    a = "A" if hc < 32 else "B"
                    ho = hc % 32
                    P.op("tensor", lambda E, a=a, ho=ho, hc=hc, pb=pb: E.matmul(
                        pb[:], AR[a][:, ho * 128:(ho + 1) * 128],
                        hbuf[:, hc * 512:hc * 512 + 512],
                        start=(hc == 0), stop=(hc == 63)),
                         reads=[R(f"ws{a}{ho // 16}"), R("hbuf")], writes=[R(pn)])
                c0 = oc * 1024 + t * 512
                P.op("vector", lambda E, c0=c0, pb=pb: E.tensor_add(resid[:, c0:c0 + 512],
                                                                    resid[:, c0:c0 + 512], pb[:]),
                     reads=[R(pn)], writes=[R("resid")])
        ln_stage()

        out_toks = []
        for tt in range(8):
            for cs in range(4):
                for k4 in range(4):
                    k = cs * 4 + k4
                    P.op("tensor", lambda E, k=k, tt=tt: E.transpose(
                        pTf[:], resid[:, k * 1024 + tt * 128:k * 1024 + tt * 128 + 128],
                        idf_sb[:]),
                         reads=[R("resid"), R("idf")], writes=[R("pTf")])
                    P.op("vector", lambda E, k4=k4: E.tensor_copy(
                        xtile[:, k4 * 128:(k4 + 1) * 128], pTf[:]),
                         reads=[R("pTf")], writes=[R("xtile")])
                    tok = P.dma("sync", out[tt * 128:(tt + 1) * 128, cs * 512:(cs + 1) * 512],
                            xtile[:], reads=[R("xtile")])
                out_toks.append(tok)
        P.op("sync", lambda E: E.nop(), extra=out_toks)
        if ctx is None:
            P.finalize(st)
    return nc


def build_fused():
    from concourse.aot_env import allow_ici_all_to_all
    nc = bass.Bass("TRN2", target_bir_lowering=False)
    P = Prog(nc)
    rg8 = [list(range(NCORES))]
    with contextlib.ExitStack() as st:
        a2a_in = nc.dram_tensor("a2a_in", [2048, 1024], BF16, kind="Internal")
        a2a_out = nc.dram_tensor("a2a_out", [2048, 1024], BF16, kind="Internal")
        wgu_q_sh = nc.dram_tensor("wgu_q_sh", [256, 16384], BF16, kind="Internal")
        wpr_q_sh = nc.dram_tensor("wpr_q_sh", [1024, 2048], BF16, kind="Internal")
        wguq_nat = nc.dram_tensor("wguq_nat", [2048, 16384], BF16, kind="Internal")
        wprq_nat = nc.dram_tensor("wprq_nat", [8192, 2048], BF16, kind="Internal")
        cc_a = st.enter_context(nc.semaphore("cc_a"))
        cc_q1 = st.enter_context(nc.semaphore("cc_q1"))
        cc_q2 = st.enter_context(nc.semaphore("cc_q2"))

        ctx = {"nc": nc, "P": P,
               "wgu_q": wgu_q_sh.ap(), "wpr_q": wpr_q_sh.ap(),
               "q_rows": (256, 1024), "q_defer": True}
        build_Q(ctx)
        qg = ctx["q_steps"]

        def seg_hook(s):
            if ctx.get("q_done"):
                return
            n = 14 if s < NSEG else 10 ** 9
            for _ in range(n):
                v = next(qg, "done")
                if v == "done":
                    ctx["q_done"] = True
                    break
        ctx["seg_hook"] = seg_hook

        def q_tick():
            if not ctx.get("q_done"):
                if next(qg, "done") == "done":
                    ctx["q_done"] = True
        ctx["q_tick"] = q_tick

        def ag_fn(E):
            E.collective_compute(
                "AllGather", ALU.bypass, replica_groups=rg8,
                ins=[wgu_q_sh.ap().opt()], outs=[wguq_nat.ap().opt()],
            ).then_inc(cc_q1, 1)
            return E.collective_compute(
                "AllGather", ALU.bypass, replica_groups=rg8,
                ins=[wpr_q_sh.ap().opt()], outs=[wprq_nat.ap().opt()],
            ).then_inc(cc_q2, 1)
        ctx["o_dst"] = lambda s, cc: a2a_in.ap()[
            256 * s + 128 * cc:256 * s + 128 * (cc + 1), :]
        build_A(ctx)
        for _ in qg:
            pass
        P.op("gpsimd", ag_fn, extra=list(ctx["q_out_toks"]))

        def a2a_fn(E):
            with allow_ici_all_to_all():
                return E.collective_compute(
                    "AllToAll", ALU.bypass, replica_groups=rg8,
                    ins=[a2a_in.ap().opt()], outs=[a2a_out.ap().opt()],
                ).then_inc(cc_a, 1)
        P.op("gpsimd", a2a_fn, extra=list(ctx["a_out_toks"]))

        wgu_v = wguq_nat.ap().rearrange("(k p) c -> p k c", p=128)
        wpr_v = wprq_nat.ap().rearrange("(hc p) c -> p hc c", p=128)
        ctx["oT_src"] = a2a_out.ap().rearrange("(k p) c -> p k c", p=128)
        ctx["gu_src"] = lambda ocb: wgu_v[:, :, ocb * 128:(ocb + 1) * 128]
        ctx["pr_src"] = lambda oc, half: wpr_v[:, half * 32:(half + 1) * 32,
                                               oc * 128:(oc + 1) * 128]
        ctx["cgate"] = []
        ctx["pre_oT"] = lambda Pr: Pr.op(
            "gpsimd", lambda E: (E.wait_ge(cc_a, 1), E.nop())[1])
        ctx["pre_gu"] = lambda Pr: Pr.op(
            "sync", lambda E: (E.wait_ge(cc_q1, 1), E.wait_ge(cc_q2, 1), E.nop())[2])
        build_B(ctx)
        P.finalize(st)
    return nc


def _fused_inputs(inputs):
    amaps = _a_inputs(inputs)
    f32 = lambda a: np.asarray(a, dtype=np.float32)
    x = f32(inputs["L0_attn_hidden_states"]).reshape(T, HID)
    bmaps = _b_host_weights(inputs)
    qmaps = _q_inputs(inputs)
    maps = []
    for c in range(NCORES):
        m = {}
        m.update(amaps[c])
        m.update(bmaps)
        m.update(qmaps[c])
        m["x_sl"] = np.ascontiguousarray(x[1024 * c:1024 * (c + 1), :])
        maps.append(m)
    return maps


def _q_inputs(inputs):
    wguT = np.ascontiguousarray(np.asarray(inputs["L1_ffn_w_gate_up"], np.float32).T)
    wprT = np.ascontiguousarray(np.asarray(inputs["L1_ffn_w_proj"], np.float32).T)
    maps = []
    for c in range(NCORES):
        maps.append({
            "wgu_sh": np.ascontiguousarray(wguT[256 * c:256 * (c + 1), :]),
            "wpr_sh": np.ascontiguousarray(wprT[1024 * c:1024 * (c + 1), :]),
        })
    return maps


def _b_host_weights(inputs):
    f32 = lambda a: np.asarray(a, dtype=np.float32)
    wo_r = np.ascontiguousarray(f32(inputs["L0_attn_Wo"]).reshape(16, 128, 16, 128)
                                .transpose(2, 1, 0, 3).reshape(16, 128, 2048))
    rg_r = np.ascontiguousarray(f32(inputs["L0_ffn_router_gate"]).T.reshape(16, 128, 8)
                                .transpose(1, 0, 2).reshape(128, 128))
    egu = f32(inputs["L0_ffn_expert_gate_up"])
    egu_g = np.ascontiguousarray(
        egu[:, :, 0::2].reshape(8, 16, 128, 8, 128).transpose(0, 3, 2, 1, 4)
        .reshape(8, 8, 128, 2048))
    egu_u = np.ascontiguousarray(
        egu[:, :, 1::2].reshape(8, 16, 128, 8, 128).transpose(0, 3, 2, 1, 4)
        .reshape(8, 8, 128, 2048))
    edn = f32(inputs["L0_ffn_expert_down"])
    edn_r = np.ascontiguousarray(
        edn.reshape(8, 8, 128, 16, 128).transpose(3, 2, 0, 1, 4).reshape(16, 128, 8192))
    bt_r = np.ascontiguousarray(f32(inputs["L1_attn_B"]).T.reshape(16, 128, 4, 256)
                                .transpose(2, 1, 0, 3).reshape(4, 128, 4096))
    ct_r = np.ascontiguousarray(f32(inputs["L1_attn_C"]).T.reshape(8, 128, 4, 512)
                                .transpose(2, 1, 0, 3).reshape(4, 128, 4096))
    dvec = np.ascontiguousarray(f32(inputs["L1_attn_D"]).reshape(16, 128).T)
    return {"wo_r": wo_r, "rg_r": rg_r, "egu_g": egu_g, "egu_u": egu_u,
            "edn_r": edn_r, "bt_r": bt_r, "ct_r": ct_r, "dvec": dvec}


def _b_inputs(inputs, o_fm, wguqT, wprqT):
    f32 = lambda a: np.asarray(a, dtype=np.float32)
    x = f32(inputs["L0_attn_hidden_states"]).reshape(T, HID)
    w = _b_host_weights(inputs)
    wguq_P = np.ascontiguousarray(
        wguqT.reshape(16, 128, 128, 128).transpose(2, 1, 0, 3).reshape(128, 128, 2048))
    wprq_P = np.ascontiguousarray(
        wprqT.reshape(64, 128, 16, 128).transpose(2, 1, 0, 3).reshape(16, 128, 8192))
    maps = []
    for c in range(NCORES):
        ts = slice(1024 * c, 1024 * (c + 1))
        oT = np.ascontiguousarray(
            o_fm[:, ts].reshape(16, 128, 1024).transpose(1, 0, 2).reshape(128, 16 * 1024))
        m = {"x_sl": np.ascontiguousarray(x[ts, :]), "oT": oT,
             "wguq_P": wguq_P, "wprq_P": wprq_P}
        m.update(w)
        maps.append(m)
    return maps


_nc_cache = {}


def _get_nc(name, builder):
    if name not in _nc_cache:
        _nc_cache[name] = builder()
    return _nc_cache[name]


def _a_inputs(inputs):
    x = np.ascontiguousarray(np.asarray(inputs["L0_attn_hidden_states"],
                                        dtype=np.float32).reshape(T, HID))
    maps = []
    for c in range(NCORES):
        cs = slice(256 * c, 256 * c + 256)
        m = {
            "x": x,
            "wq": np.ascontiguousarray(inputs["L0_attn_Wq"][:, cs]),
            "wk": np.ascontiguousarray(inputs["L0_attn_Wk"][:, cs]),
            "wv": np.ascontiguousarray(inputs["L0_attn_Wv"][:, cs]),
            "ww": np.ascontiguousarray(inputs["L0_attn_Ww"][:, cs]),
            "cq": np.ascontiguousarray(
                inputs["L0_attn_cq"][cs, :].reshape(2, 128, 4).transpose(1, 0, 2).reshape(128, 8)),
            "ck": np.ascontiguousarray(
                inputs["L0_attn_ck"][cs, :].reshape(2, 128, 4).transpose(1, 0, 2).reshape(128, 8)),
            "cv": np.ascontiguousarray(
                inputs["L0_attn_cv"][cs, :].reshape(2, 128, 4).transpose(1, 0, 2).reshape(128, 8)),
            "wf1": np.ascontiguousarray(inputs["L0_attn_Wf1"]),
            "wf2": np.ascontiguousarray(inputs["L0_attn_Wf2"][:, cs]),
            "wg1": np.ascontiguousarray(inputs["L0_attn_Wg1"]),
            "wg2": np.ascontiguousarray(inputs["L0_attn_Wg2"][:, cs]),
            "wbp": np.concatenate(
                [np.asarray(inputs["L0_attn_Wb"][:, 2 * c + j]).reshape(16, 128).T
                 for j in (0, 1)], axis=1),
            "bg2": np.ascontiguousarray(
                np.asarray(inputs["L0_attn_bg2"][cs]).reshape(2, 128).T),
            "normw": np.ascontiguousarray(inputs["L0_attn_norm_w"].reshape(128, 1)),
        }
        maps.append({k: np.asarray(v, dtype=np.float32) for k, v in m.items()})
    return maps


def run_launch_A(inputs):
    nc = _get_nc("A", build_A)
    maps = _a_inputs(inputs)
    res = run_bass_kernel_spmd(nc, maps, core_ids=list(range(NCORES)))
    o = np.concatenate([res.results[c]["o_fm"] for c in range(NCORES)], axis=0)
    return o


def run_launch_Q(inputs):
    nc = _get_nc("Q", build_Q)
    maps = _q_inputs(inputs)
    res = run_bass_kernel_spmd(nc, maps, core_ids=list(range(NCORES)))
    wguqT = np.concatenate([res.results[c]["wgu_q"] for c in range(NCORES)], axis=0)
    wprqT = np.concatenate([res.results[c]["wpr_q"] for c in range(NCORES)], axis=0)
    return wguqT, wprqT


def run_launch_B(inputs, o_fm, wguqT, wprqT):
    nc = _get_nc("B", build_B)
    maps = _b_inputs(inputs, o_fm, wguqT, wprqT)
    res = run_bass_kernel_spmd(nc, maps, core_ids=list(range(NCORES)))
    return np.concatenate([res.results[c]["out"] for c in range(NCORES)], axis=0)


FUSED = True


def kernel(**inputs):
    if FUSED:
        nc = _get_nc("F", build_fused)
        maps = _fused_inputs(inputs)
        res = run_bass_kernel_spmd(nc, maps, core_ids=list(range(NCORES)))
        y = np.concatenate([res.results[c]["out"] for c in range(NCORES)], axis=0)
        return y.reshape(1, T, HID).astype(np.float32)
    o_fm = run_launch_A(inputs)
    wguqT, wprqT = run_launch_Q(inputs)
    y = run_launch_B(inputs, o_fm, wguqT, wprqT)
    return y.reshape(1, T, HID).astype(np.float32)


# version 14 (final)
# speedup vs baseline: 1.0273x; 1.0273x over previous
import math
import contextlib
import numpy as np
import ml_dtypes
import concourse.bass as bass
import concourse.mybir as mybir
from concourse.bass_utils import run_bass_kernel_spmd

F32 = mybir.dt.float32
BF16 = mybir.dt.bfloat16
AF = mybir.ActivationFunctionType
ALU = mybir.AluOpType
AX = mybir.AxisListType.X
bfnp = ml_dtypes.bfloat16

T, HID = 8192, 2048
H, DK, DV, M, KW = 16, 128, 128, 128, 4
SCALE = DK ** -0.5
NCORES = 8
CHUNK = 128
SEG = 1024
NSEG = T // SEG
CPS = SEG // CHUNK
DBG_PROJ = True
DBG_PARTS = 31
DBG_SCAN = True

ENGINES = ("tensor", "vector", "scalar", "gpsimd", "sync")


class Res:
    __slots__ = ("w", "readers", "excl")

    def __init__(self, excl=False):
        self.w = None
        self.readers = {}
        self.excl = excl

    def rd(self):
        return [self.w] if self.w is not None else []

    def wr(self):
        out = [self.w] if self.w is not None else []
        out.extend(self.readers.values())
        return out


class Prog:

    NLANE = 4

    def __init__(self, nc):
        self.nc = nc
        self.ops = {e: [] for e in ENGINES}
        self.dma_lane = {e: {} for e in ENGINES}

    def op(self, eng, fn, reads=(), writes=(), extra=(), dma=False):
        reads = tuple(reads)
        writes = tuple(writes) + tuple(r for r in reads if r.excl)
        reads = tuple(r for r in reads if not r.excl)
        waits = []
        for r in reads:
            waits.extend(r.rd())
        for r in writes:
            waits.extend(r.wr())
        waits.extend(t for t in extra if t is not None)
        best = {}
        for (pe, ps) in waits:
            if pe == eng == "tensor":
                continue
            lane = self.dma_lane[pe].get(ps)
            key = (pe, "c") if lane is None else (pe, "d%d" % (lane % self.NLANE))
            if best.get(key, (None, -1))[1] < ps:
                best[key] = (pe, ps)
        self.ops[eng].append((fn, sorted(best.values()), 16 if dma else 1))
        tok = (eng, len(self.ops[eng]) - 1)
        if dma:
            self.dma_lane[eng][tok[1]] = len(self.dma_lane[eng])
        for r in reads:
            r.readers[eng] = tok
        for r in writes:
            r.w = tok
            r.readers = {}
        return tok

    def barrier(self):
        toks = [(e, len(self.ops[e]) - 1) for e in ENGINES if self.ops[e]]
        for e in ENGINES:
            if self.ops[e]:
                self.op(e, lambda E: E.nop(), extra=list(toks))

    def dma(self, eng, out, in_, reads=(), writes=(), extra=(), **kw):
        return self.op(eng, lambda E: E.dma_start(out=out, in_=in_, **kw),
                       reads=reads, writes=writes, extra=extra, dma=True)

    def finalize(self, stack):
        nc = self.nc
        used = {e: set() for e in ENGINES}
        for e in ENGINES:
            for i, (_, waits, amt) in enumerate(self.ops[e]):
                if amt == 16:
                    used[e].add(i)
                for pe, ps in waits:
                    used[pe].add(ps)
        NLANE = self.NLANE
        sems = {}
        val_at = {}
        for e in ENGINES:
            cum = {"c": 0}
            for l in range(NLANE):
                cum[f"d{l}"] = 0
            va = {}
            incs = used[e]
            kinds = set()
            for i, (_, _, amt) in enumerate(self.ops[e]):
                if amt == 16:
                    kind = f"d{self.dma_lane[e][i] % NLANE}"
                else:
                    kind = "c"
                if i in incs:
                    cum[kind] += amt
                    kinds.add(kind)
                va[i] = (kind, cum[kind])
            val_at[e] = va
            for kind in kinds:
                sems[(e, kind)] = stack.enter_context(nc.semaphore(f"ms_{e}_{kind}"))
        block = stack.enter_context(nc.Block())

        def make_runner(e):
            ops = self.ops[e]
            incs = used[e]
            va = val_at[e]

            def run(E):
                waited = {}
                prev_lane = {}
                for i, (fn, waits, amt) in enumerate(ops):
                    if amt == 16:
                        kind, v = va[i]
                        pv = prev_lane.get(kind, 0)
                        if pv > waited.get((e, kind), 0):
                            E.wait_ge(sems[(e, kind)], pv)
                            waited[(e, kind)] = pv
                    for pe, ps in waits:
                        kind2, v2 = val_at[pe][ps]
                        if waited.get((pe, kind2), 0) >= v2:
                            continue
                        E.wait_ge(sems[(pe, kind2)], v2)
                        waited[(pe, kind2)] = v2
                    inst = fn(E)
                    if i in incs:
                        inst.then_inc(sems[(e, va[i][0])], amt)
                    if amt == 16:
                        prev_lane[va[i][0]] = va[i][1]
            return run

        for e in ENGINES:
            if self.ops[e]:
                getattr(block, e)(make_runner(e))
        return sems, val_at


def ceil_div(a, b):
    return (a + b - 1) // b



def build_A(ctx=None):
    if ctx is None:
        nc = bass.Bass("TRN2", target_bir_lowering=False)
    else:
        nc = ctx["nc"]
    x = nc.dram_tensor("x", [T, HID], F32, kind="ExternalInput").ap()
    wq = nc.dram_tensor("wq", [HID, 256], F32, kind="ExternalInput").ap()
    wk = nc.dram_tensor("wk", [HID, 256], F32, kind="ExternalInput").ap()
    wv = nc.dram_tensor("wv", [HID, 256], F32, kind="ExternalInput").ap()
    ww = nc.dram_tensor("ww", [HID, 256], F32, kind="ExternalInput").ap()
    cq = nc.dram_tensor("cq", [128, 8], F32, kind="ExternalInput").ap()
    ck = nc.dram_tensor("ck", [128, 8], F32, kind="ExternalInput").ap()
    cv = nc.dram_tensor("cv", [128, 8], F32, kind="ExternalInput").ap()
    wf1 = nc.dram_tensor("wf1", [HID, 128], F32, kind="ExternalInput").ap()
    wf2 = nc.dram_tensor("wf2", [128, 256], F32, kind="ExternalInput").ap()
    wg1 = nc.dram_tensor("wg1", [HID, 128], F32, kind="ExternalInput").ap()
    wg2 = nc.dram_tensor("wg2", [128, 256], F32, kind="ExternalInput").ap()
    wbp = nc.dram_tensor("wbp", [128, 32], F32, kind="ExternalInput").ap()
    bg2 = nc.dram_tensor("bg2", [128, 2], F32, kind="ExternalInput").ap()
    normw = nc.dram_tensor("normw", [128, 1], F32, kind="ExternalInput").ap()
    if ctx is None:
        o_fm = nc.dram_tensor("o_fm", [256, T], BF16, kind="ExternalOutput").ap()
        o_dst = lambda s, cc: o_fm[cc * 128:(cc + 1) * 128, s * SEG:(s + 1) * SEG]
    else:
        o_dst = ctx["o_dst"]
    xbf = nc.dram_tensor("xbf", [T, HID], BF16, kind="Internal").ap()

    triu = np.triu(np.ones((CHUNK, CHUNK), dtype=np.float32) * SCALE * 0 + 1.0)
    triu_h = nc.inline_tensor(triu.astype(bfnp), name="a_triu")
    ident_h = nc.inline_tensor(np.eye(128, dtype=bfnp), name="a_ident")
    ones_h = nc.inline_tensor(np.ones((128, 1), dtype=bfnp), name="a_ones_col")
    onesr_h = nc.inline_tensor(np.ones((1, 128), dtype=bfnp), name="a_ones_row")

    with contextlib.ExitStack() as st:
        sb = lambda name, shape, dt: st.enter_context(nc.sbuf_tensor("a_" + name, shape, dt))
        ps = lambda name, shape, dt: st.enter_context(nc.psum_tensor("a_" + name, shape, dt))

        wq_sb = sb("wq_sb", [128, 16 * 256], BF16)
        wk_sb = sb("wk_sb", [128, 16 * 256], BF16)
        wv_sb = sb("wv_sb", [128, 16 * 256], BF16)
        ww_sb = sb("ww_sb", [128, 16 * 256], BF16)
        wf1_sb = sb("wf1_sb", [128, 16 * 128], BF16)
        wg1_sb = sb("wg1_sb", [128, 16 * 128], BF16)
        wf2_sb = sb("wf2_sb", [128, 256], BF16)
        wg2_sb = sb("wg2_sb", [128, 256], BF16)
        wb_sb = sb("wb_sb", [128, 32], BF16)
        cq_sb = sb("cq_sb", [128, 8], F32)
        ck_sb = sb("ck_sb", [128, 8], F32)
        cv_sb = sb("cv_sb", [128, 8], F32)
        bg2_sb = sb("bg2_sb", [128, 2], F32)
        normw_sb = sb("normw_sb", [128, 1], F32)
        triu_sb = sb("triu_sb", [128, 128], BF16)
        id_sb = sb("id_sb", [128, 128], BF16)
        ones_sb = sb("ones_sb", [128, 1], BF16)
        onesr_sb = sb("onesr_sb", [1, 128], BF16)

        xt = sb("xt", [128, 16 * SEG], BF16)
        qT = sb("qT", [128, 2 * SEG], BF16)
        kT = sb("kT", [128, 2 * SEG], BF16)
        k_tm = sb("k_tm", [128, 2 * SEG], BF16)
        v_tm = sb("v_tm", [128, 2 * SEG], BF16)
        uT = sb("uT", [128, 2 * SEG], BF16)
        u_tm = sb("u_tm", [128, 2 * SEG], BF16)
        cumc = sb("cumc", [128, 2 * SEG], F32)
        Gt = sb("Gt", [128, 2 * SEG], BF16)
        o_seg = sb("o_seg", [128, 2 * SEG], BF16)
        beta0 = sb("beta0", [1, 512], F32)
        beta1 = sb("beta1", [1, 512], F32)
        refc = sb("refc", [128, 2 * CPS], F32)
        lastc = sb("lastc", [128, 2 * CPS], F32)
        eref = sb("eref", [128, 2 * CPS], F32)
        elmr = sb("elmr", [128, 2 * CPS], F32)
        glast = sb("glast", [128, 2 * CPS], F32)

        pre_b = [sb(f"pre{i}", [128, 2 * (512 + 3)], F32) for i in range(2)]
        halo = sb("halo", [128, 2 * 4 * 3], F32)
        f1_bf = sb("f1_bf", [128, 512], BF16)
        wtmp_b = [sb(f"wtmp{i}", [128, 2 * 512], F32) for i in range(2)]
        sq_bf = sb("sq_bf", [128, 512], BF16)
        sq_bf2 = sb("sq_bf2", [128, 512], BF16)
        row_a = sb("row_a", [1, 512], F32)
        row_b = sb("row_b", [1, 512], F32)
        row_c = sb("row_c", [1, 512], BF16)
        g_abs = sb("g_abs", [128, 512], F32)
        g_ln = sb("g_ln", [128, 512], F32)
        g_min = sb("g_min", [128, 512], F32)

        SkT = [sb(f"SkT{h}", [128, 2 * 128], F32) for h in range(2)]
        SvT = [sb(f"SvT{h}", [128, 2 * 128], F32) for h in range(2)]
        Sk_dm = [sb(f"Skdm{h}", [128, 128], BF16) for h in range(2)]
        SvT_sc = [sb(f"Svsc{h}", [128, 128], BF16) for h in range(2)]
        atm = [sb(f"atm{h}", [128, 128], BF16) for h in range(2)]
        btm = [sb(f"btm{h}", [128, 128], BF16) for h in range(2)]
        at_c = [sb(f"at_c{h}", [128, 128], BF16) for h in range(2)]
        atT = [sb(f"atT{h}", [128, 128], BF16) for h in range(2)]
        s_sb = [sb(f"s_sb{h}", [128, 128], F32) for h in range(2)]
        e_sb = [sb(f"e_sb{h}", [128, 128], F32) for h in range(2)]
        gres = sb("gres", [128, 2 * SEG], BF16)
        nmax = [sb(f"nmax{h}", [128, 1], F32) for h in range(2)]
        rsum = [sb(f"rsum{h}", [128, 1], F32) for h in range(2)]
        rinv = [sb(f"rinv{h}", [128, 1], F32) for h in range(2)]
        orow = [sb(f"orow{h}", [1, 128], F32) for h in range(2)]
        orow2 = [sb(f"orow2{h}", [1, 128], F32) for h in range(2)]
        orow_bf = [sb(f"orow_bf{h}", [1, 128], BF16) for h in range(2)]
        osq = [sb(f"osq{h}", [128, 128], BF16) for h in range(2)]
        o1 = [sb(f"o1{h}", [128, 128], F32) for h in range(2)]
        tsk = [sb(f"tsk{h}", [128, 128], F32) for h in range(2)]
        tsv = [sb(f"tsv{h}", [128, 128], F32) for h in range(2)]
        vcol = sb("vcol", [128, 1], F32)
        vcol2 = sb("vcol2", [128, 1], F32)

        bkA = [ps(f"bkA{h}", [128, 512], F32) for h in range(2)]
        bkB = [ps(f"bkB{h}", [128, 512], F32) for h in range(2)]
        bkT = [ps(f"bkT{h}", [128, 256], BF16) for h in range(2)]
        pA, p1 = bkA[0], bkA[1]
        pB = bkB[0]
        pR = bkB[1]
        pTf = bkB[0][:, 0:128]

        P = Prog(nc) if ctx is None else ctx["P"]
        rx = {name: Res() for name in (
            "wq", "wk", "wv", "ww", "wf1", "wg1", "wf2", "wg2", "wb", "cq",
            "ck", "cv", "bg2", "normw", "triu", "id", "ones", "onesr",
            "xt", "qT", "kT", "k_tm", "v_tm", "uT", "u_tm", "cumc", "Gt",
            "o_seg", "beta0", "beta1", "refc", "lastc", "eref", "elmr",
            "glast", "pre0", "pre1", "halo", "f1", "wtmp0", "wtmp1", "sq", "rowa", "rowb",
            "rowc", "gabs", "gln", "gmin",
            "gres", "vcol", "vcol2", "xbf",
        )}
        for _h in range(2):
            for _n in ("atm", "btm", "atc", "atT", "s", "e", "nmax", "rsum",
                       "rinv", "orow", "orow2", "orowbf", "osq", "o1", "tsk", "tsv"):
                rx[f"{_n}{_h}"] = Res()
        for n in ("bkA0", "bkA1", "bkB0", "bkB1", "bkT0", "bkT1"):
            rx[n] = Res(excl=True)
        rx["pA"] = rx["bkA0"]; rx["p1"] = rx["bkA1"]
        rx["pB"] = rx["pTf"] = rx["bkB0"]; rx["pR"] = rx["bkB1"]
        for h in range(2):
            rx[f"SkT{h}"] = Res(); rx[f"SvT{h}"] = Res()
            rx[f"Skdm{h}"] = Res(); rx[f"Svsc{h}"] = Res()
        R = lambda n: rx[n]

        def wload(dst, src, res, kk):
            P.dma("gpsimd", dst[:].rearrange("p (k c) -> p k c", k=kk),
                  src.rearrange("(k p) c -> p k c", p=128), writes=[R(res)])
        wload(wq_sb, wq, "wq", 16)
        wload(wk_sb, wk, "wk", 16)
        wload(wv_sb, wv, "wv", 16)
        wload(ww_sb, ww, "ww", 16)
        wload(wf1_sb, wf1, "wf1", 16)
        wload(wg1_sb, wg1, "wg1", 16)
        P.dma("gpsimd", wf2_sb[:], wf2[:, :], writes=[R("wf2")])
        P.dma("gpsimd", wg2_sb[:], wg2[:, :], writes=[R("wg2")])
        P.dma("gpsimd", wb_sb[:], wbp[:, :], writes=[R("wb")])
        P.dma("gpsimd", cq_sb[:], cq[:, :], writes=[R("cq")])
        P.dma("gpsimd", ck_sb[:], ck[:, :], writes=[R("ck")])
        P.dma("gpsimd", cv_sb[:], cv[:, :], writes=[R("cv")])
        P.dma("gpsimd", bg2_sb[:], bg2[:, :], writes=[R("bg2")])
        P.dma("gpsimd", normw_sb[:], normw[:, :], writes=[R("normw")])
        P.dma("gpsimd", triu_sb[:], triu_h.ap()[:, :], writes=[R("triu")])
        P.dma("gpsimd", id_sb[:], ident_h.ap()[:, :], writes=[R("id")])
        P.dma("gpsimd", ones_sb[:], ones_h.ap()[:, :], writes=[R("ones")])
        P.dma("gpsimd", onesr_sb[:], onesr_h.ap()[:, :], writes=[R("onesr")])
        for h in range(2):
            P.op("vector", lambda E, t=SkT[h]: E.memset(t[:], 0.0), writes=[R(f"SkT{h}")])
            P.op("vector", lambda E, t=SvT[h]: E.memset(t[:], 0.0), writes=[R(f"SvT{h}")])
            P.op("vector", lambda E, t=Sk_dm[h]: E.memset(t[:], 0.0), writes=[R(f"Skdm{h}")])
            P.op("vector", lambda E, t=SvT_sc[h]: E.memset(t[:], 0.0), writes=[R(f"Svsc{h}")])
        P.op("vector", lambda E: E.memset(halo[:], 0.0), writes=[R("halo")])
        for h in range(2):
            P.op("vector", lambda E, t=btm[h]: E.memset(t[:], 0.0), writes=[R(f"btm{h}")])

        xbf_tok = []
        for s in range(NSEG):
            t = P.dma("gpsimd", xbf[s * SEG:(s + 1) * SEG, :], x[s * SEG:(s + 1) * SEG, :],
                      writes=[R("xbf")])
            xbf_tok.append(t)

        out_toks = []

        def proj_tile(s, t, tok0, TL):
            xs = lambda k: xt[:, k * SEG + t * 512: k * SEG + t * 512 + TL]
            pidx = [0]
            for prji, (name, wsb, wres, csb, cres) in enumerate(() if not (DBG_PARTS & 1) else (
                    ("q", wq_sb, "wq", cq_sb, "cq"), ("k", wk_sb, "wk", ck_sb, "ck"),
                    ("v", wv_sb, "wv", cv_sb, "cv"), ("w", ww_sb, "ww", cv_sb, "cv"))):
                pre = pre_b[prji % 2]
                wtmp = wtmp_b[prji % 2]
                rpre = "pre%d" % (prji % 2)
                rwt = "wtmp%d" % (prji % 2)
                for cc in range(2):
                    pb, pn = (pA, "pA") if pidx[0] % 2 == 0 else (p1, "p1")
                    pidx[0] += 1
                    for k in range(16):
                        P.op("tensor",
                             lambda E, pre=pre, wtmp=wtmp, k=k, cc=cc, wsb=wsb, xs=xs, pb=pb: E.matmul(
                                 pb[:, :TL], wsb[:, k * 256 + cc * 128: k * 256 + cc * 128 + 128],
                                 xs(k), start=(k == 0), stop=(k == 15)),
                             reads=[R(wres), R("xt")],
                             writes=[R(pn)])
                    pcol = cc * 515
                    P.op("vector",
                         lambda E, pre=pre, wtmp=wtmp, pcol=pcol, pb=pb: E.tensor_copy(pre[:, pcol + 3: pcol + 3 + TL], pb[:, :TL]),
                         reads=[R(pn)], writes=[R(rpre)])
                pi = {"q": 0, "k": 1, "v": 2, "w": 3}[name]
                for cc in range(2):
                    pcol = cc * 515
                    P.op("gpsimd",
                         lambda E, pre=pre, wtmp=wtmp, pcol=pcol, pi=pi, cc=cc: E.tensor_copy(
                             pre[:, pcol:pcol + 3], halo[:, (cc * 4 + pi) * 3:(cc * 4 + pi) * 3 + 3]),
                         reads=[R("halo")], writes=[R(rpre)])
                    wk0 = cc * 512
                    for i in range(KW):
                        sh = 3 - (KW - 1 - i)
                        if i == 0:
                            P.op("vector",
                                 lambda E, pre=pre, wtmp=wtmp, pcol=pcol, sh=sh, wk0=wk0, csb=csb, cc=cc, i=i:
                                 E.tensor_scalar(wtmp[:, wk0:wk0 + TL],
                                                 pre[:, pcol + sh:pcol + sh + TL],
                                                 csb[:, cc * 4 + i:cc * 4 + i + 1], None, ALU.mult),
                                 reads=[R(rpre), R(cres)], writes=[R(rwt)])
                        else:
                            P.op("vector",
                                 lambda E, pre=pre, wtmp=wtmp, pcol=pcol, sh=sh, wk0=wk0, csb=csb, cc=cc, i=i:
                                 E.scalar_tensor_tensor(wtmp[:, wk0:wk0 + TL],
                                                        pre[:, pcol + sh:pcol + sh + TL],
                                                        csb[:, cc * 4 + i:cc * 4 + i + 1],
                                                        wtmp[:, wk0:wk0 + TL], ALU.mult, ALU.add),
                                 reads=[R(rpre), R(cres)], writes=[R(rwt)])
                    P.op("gpsimd",
                         lambda E, pre=pre, wtmp=wtmp, pcol=pcol, pi=pi, cc=cc: E.tensor_copy(
                             halo[:, (cc * 4 + pi) * 3:(cc * 4 + pi) * 3 + 3],
                             pre[:, pcol + TL:pcol + TL + 3]),
                         reads=[R(rpre)], writes=[R("halo")])
                    dst_map = {"q": (qT, "qT"), "k": (kT, "kT")}
                    if name in ("q", "k"):
                        dsb, dres = dst_map[name]
                        P.op("scalar",
                             lambda E, pre=pre, wtmp=wtmp, dsb=dsb, cc=cc, wk0=wk0: E.activation(
                                 out=dsb[:, cc * SEG + tok0:cc * SEG + tok0 + TL],
                                 in_=wtmp[:, wk0:wk0 + TL], func=AF.Silu),
                             reads=[R(rwt)], writes=[R(dres)])
                    elif name == "v":
                        P.op("scalar",
                             lambda E, pre=pre, wtmp=wtmp, cc=cc, wk0=wk0: E.activation(
                                 out=sq_bf[:, :TL], in_=wtmp[:, wk0:wk0 + TL], func=AF.Silu),
                             reads=[R(rwt)], writes=[R("sq")])
                        for b in range(TL // 128):
                            bt, btn = bkT[cc], f"bkT{cc}"
                            P.op("tensor",
                                 lambda E, b=b, bt=bt: E.transpose(bt[:, 0:128], sq_bf[:, b * 128:(b + 1) * 128], id_sb[:]),
                                 reads=[R("sq"), R("id")], writes=[R(btn)])
                            P.op("scalar",
                                 lambda E, b=b, cc=cc, bt=bt: E.activation(
                                     out=v_tm[:, cc * SEG + tok0 + b * 128: cc * SEG + tok0 + b * 128 + 128],
                                     in_=bt[:, 0:128], func=AF.Copy),
                                 reads=[R(btn)], writes=[R("v_tm")])
                    else:
                        P.op("scalar",
                             lambda E, pre=pre, wtmp=wtmp, cc=cc, wk0=wk0: E.activation(
                                 out=wtmp[:, wk0:wk0 + TL], in_=wtmp[:, wk0:wk0 + TL], func=AF.Silu),
                             reads=[R(rwt)], writes=[R(rwt)])

            for k in range(16 if DBG_PARTS & 2 else 0):
                P.op("tensor",
                     lambda E, k=k, xs=xs: E.matmul(pA[:, :TL], wf1_sb[:, k * 128:(k + 1) * 128],
                                                    xs(k), start=(k == 0), stop=(k == 15)),
                     reads=[R("wf1"), R("xt")], writes=[R("pA")])
            if DBG_PARTS & 2:
              P.op("scalar", lambda E: E.activation(out=f1_bf[:, :TL], in_=pA[:, :TL],
                                                    func=AF.Copy),
                 reads=[R("pA")], writes=[R("f1")])
            for cc in range(2 if DBG_PARTS & 2 else 0):
                P.op("tensor",
                     lambda E, cc=cc: E.matmul(p1[:, :TL], wf2_sb[:, cc * 128:(cc + 1) * 128],
                                               f1_bf[:, :TL], start=True, stop=True),
                     reads=[R("wf2"), R("f1")], writes=[R("p1")])
                P.op("scalar", lambda E: E.activation(out=g_abs[:, :TL], in_=p1[:, :TL], func=AF.Abs),
                     reads=[R("p1")], writes=[R("gabs")])
                P.op("scalar", lambda E: E.activation(out=g_ln[:, :TL], in_=g_abs[:, :TL],
                                                      func=AF.Exp, scale=-1.0),
                     reads=[R("gabs")], writes=[R("gln")])
                P.op("scalar", lambda E: E.activation(out=g_ln[:, :TL], in_=g_ln[:, :TL],
                                                      func=AF.Ln, bias=biasc[:, 2:3]),
                     reads=[R("gln"), R("biasc")], writes=[R("gln")])
                P.op("vector", lambda E: E.tensor_single_scalar(g_min[:, :TL], p1[:, :TL], 0.0, ALU.min),
                     reads=[R("p1")], writes=[R("gmin")])
                P.op("vector", lambda E: E.tensor_sub(g_abs[:, :TL], g_min[:, :TL], g_ln[:, :TL]),
                     reads=[R("gmin"), R("gln")], writes=[R("gabs")])
                for b in range(TL // 128):
                    c0 = cc * SEG + tok0 + b * 128
                    P.op("vector",
                         lambda E, b=b, c0=c0: E.tensor_tensor_scan(
                             cumc[:, c0:c0 + 128], g_abs[:, b * 128:(b + 1) * 128],
                             g_abs[:, b * 128:(b + 1) * 128], 0.0, ALU.add, ALU.bypass),
                         reads=[R("gabs")], writes=[R("cumc")])

            for hh, (bres, bsb) in enumerate((() if not (DBG_PARTS & 4) else (("beta0", beta0), ("beta1", beta1)))):
                for k in range(16):
                    P.op("tensor",
                         lambda E, k=k, hh=hh, xs=xs: E.matmul(
                             pB[0:1, :TL], wb_sb[:, hh * 16 + k:hh * 16 + k + 1], xs(k),
                             start=(k == 0), stop=(k == 15)),
                         reads=[R("wb"), R("xt")], writes=[R("pB")])
                P.op("scalar",
                     lambda E, bsb=bsb: E.activation(out=bsb[:, :TL], in_=pB[0:1, :TL],
                                                     func=AF.Sigmoid),
                     reads=[R("pB")], writes=[R(bres)])

            for k in range(16 if DBG_PARTS & 8 else 0):
                P.op("tensor",
                     lambda E, k=k, xs=xs: E.matmul(pA[:, :TL], wg1_sb[:, k * 128:(k + 1) * 128],
                                                    xs(k), start=(k == 0), stop=(k == 15)),
                     reads=[R("wg1"), R("xt")], writes=[R("pA")])
            if DBG_PARTS & 8:
              P.op("scalar", lambda E: E.activation(out=f1_bf[:, :TL], in_=pA[:, :TL],
                                                    func=AF.Copy),
                 reads=[R("pA")], writes=[R("f1")])
            for cc in range(2 if DBG_PARTS & 8 else 0):
                P.op("tensor",
                     lambda E, cc=cc: E.matmul(p1[:, :TL], wg2_sb[:, cc * 128:(cc + 1) * 128],
                                               f1_bf[:, :TL], start=True, stop=True),
                     reads=[R("wg2"), R("f1")], writes=[R("p1")])
                P.op("scalar",
                     lambda E, cc=cc: E.activation(out=g_min[:, :TL], in_=p1[:, :TL],
                                                   func=AF.Sigmoid, bias=bg2_sb[:, cc:cc + 1]),
                     reads=[R("p1"), R("bg2")], writes=[R("gmin")])
                P.op("vector",
                     lambda E, cc=cc: E.tensor_scalar(Gt[:, cc * SEG + tok0:cc * SEG + tok0 + TL],
                                                      g_min[:, :TL], normw_sb[:, 0:1], None, ALU.mult),
                     reads=[R("gmin"), R("normw")], writes=[R("Gt")])

            for cc in range(2 if DBG_PARTS & 16 else 0):
                wk0 = cc * 512
                brow = beta0 if cc == 0 else beta1
                bres = "beta0" if cc == 0 else "beta1"
                P.op("vector", lambda E, wk0=wk0: E.tensor_mul(sq_bf[:, :TL], wtmp_b[1][:, wk0:wk0 + TL],
                                                               wtmp_b[1][:, wk0:wk0 + TL]),
                     reads=[R("wtmp1")], writes=[R("sq")])
                P.op("tensor", lambda E: E.matmul(pB[0:1, :TL], ones_sb[:], sq_bf[:, :TL],
                                                  start=True, stop=True),
                     reads=[R("sq"), R("ones")], writes=[R("pB")])
                P.op("scalar", lambda E: E.activation(out=row_a[:, :TL], in_=pB[0:1, :TL],
                                                      func=AF.Ln, bias=biasc[0:1, 0:1]),
                     reads=[R("pB"), R("biasc")], writes=[R("rowa")])
                P.op("scalar", lambda E: E.activation(out=row_b[:, :TL], in_=row_a[:, :TL],
                                                      func=AF.Exp, scale=-0.5),
                     reads=[R("rowa")], writes=[R("rowb")])
                P.op("vector", lambda E, brow=brow: E.tensor_mul(row_c[:, :TL], row_b[:, :TL],
                                                                 brow[:, :TL]),
                     reads=[R("rowb"), R(bres)], writes=[R("rowc")])
                P.op("tensor", lambda E: E.matmul(pR[:, :TL], onesr_sb[:], row_c[:, :TL],
                                                  start=True, stop=True),
                     reads=[R("rowc"), R("onesr")], writes=[R("pR")])
                for b in range(TL // 128):
                    c0 = cc * SEG + tok0 + b * 128
                    ci = cc * CPS + (tok0 + b * 128) // 128
                    P.op("vector",
                         lambda E, c0=c0, ci=ci: E.tensor_copy(lastc[:, ci:ci + 1], cumc[:, c0 + 127:c0 + 128]),
                         reads=[R("cumc")], writes=[R("lastc")])
                    P.op("vector",
                         lambda E, ci=ci: E.tensor_scalar(refc[:, ci:ci + 1], lastc[:, ci:ci + 1],
                                                          0.5, None, ALU.mult),
                         reads=[R("lastc")], writes=[R("refc")])
                    P.op("scalar",
                         lambda E, ci=ci: E.activation(out=eref[:, ci:ci + 1], in_=refc[:, ci:ci + 1],
                                                       func=AF.Exp),
                         reads=[R("refc")], writes=[R("eref")])
                    P.op("scalar",
                         lambda E, ci=ci: E.activation(out=glast[:, ci:ci + 1], in_=lastc[:, ci:ci + 1],
                                                       func=AF.Exp),
                         reads=[R("lastc")], writes=[R("glast")])
                    P.op("scalar",
                         lambda E, ci=ci: E.activation(out=elmr[:, ci:ci + 1], in_=lastc[:, ci:ci + 1],
                                                       func=AF.Exp, scale=0.5),
                         reads=[R("lastc")], writes=[R("elmr")])
                    P.op("vector",
                         lambda E, c0=c0, ci=ci: E.tensor_scalar(cumc[:, c0:c0 + 128], cumc[:, c0:c0 + 128],
                                                                 refc[:, ci:ci + 1], None, ALU.subtract),
                         reads=[R("refc")], writes=[R("cumc")])
                    P.op("tensor",
                         lambda E, c0=c0: E.transpose(pTf[:], cumc[:, c0:c0 + 128], id_f32_sb[:]),
                         reads=[R("cumc"), R("idf")], writes=[R("pTf")])
                    P.op("scalar",
                         lambda E, c0=c0: E.activation(out=gres[:, c0:c0 + 128], in_=pTf[:],
                                                       func=AF.Exp),
                         reads=[R("pTf")], writes=[R("gres")])
                    P.op("scalar",
                         lambda E, c0=c0, b=b: E.activation(out=g_ln[:, b * 128:(b + 1) * 128],
                                                            in_=cumc[:, c0:c0 + 128], func=AF.Exp, scale=-1.0),
                         reads=[R("cumc")], writes=[R("gln")])
                    P.op("vector",
                         lambda E, wk0=wk0, b=b: E.tensor_mul(g_min[:, b * 128:(b + 1) * 128],
                                                              wtmp_b[1][:, wk0 + b * 128:wk0 + (b + 1) * 128],
                                                              pR[:, b * 128:(b + 1) * 128]),
                         reads=[R("wtmp1"), R("pR")], writes=[R("gmin")])
                    P.op("vector",
                         lambda E, c0=c0, b=b: E.tensor_mul(uT[:, c0:c0 + 128],
                                                            g_min[:, b * 128:(b + 1) * 128],
                                                            g_ln[:, b * 128:(b + 1) * 128]),
                         reads=[R("gmin"), R("gln")], writes=[R("uT")])
                    bt, btn = bkT[cc], f"bkT{cc}"
                    P.op("tensor",
                         lambda E, c0=c0, bt=bt: E.transpose(bt[:, 0:128], uT[:, c0:c0 + 128], id_sb[:]),
                         reads=[R("uT"), R("id")], writes=[R(btn)])
                    P.op("scalar",
                         lambda E, c0=c0, bt=bt: E.activation(out=u_tm[:, c0:c0 + 128],
                                                              in_=bt[:, 0:128], func=AF.Copy),
                         reads=[R(btn)], writes=[R("u_tm")])
                    P.op("tensor",
                         lambda E, c0=c0, bt=bt: E.transpose(bt[:, 128:256], kT[:, c0:c0 + 128], id_sb[:]),
                         reads=[R("kT"), R("id")], writes=[R(btn)])
                    P.op("scalar",
                         lambda E, c0=c0, bt=bt: E.activation(out=k_tm[:, c0:c0 + 128],
                                                              in_=bt[:, 128:256], func=AF.Copy),
                         reads=[R(btn)], writes=[R("k_tm")])

        def scan_chunk(s, h, b):
            c0 = h * SEG + b * 128
            ci = h * CPS + b
            glob_chunk = s * CPS + b
            pp = glob_chunk % 2
            nsk, nsv = f"SkT{h}", f"SvT{h}"
            ndm, nsc = f"Skdm{h}", f"Svsc{h}"
            bA, nA = bkA[h], f"bkA{h}"
            bB, nB = bkB[h], f"bkB{h}"
            bT, nT = bkT[h], f"bkT{h}"
            gsl = gres[:, c0:c0 + 128]
            P.op("tensor", lambda E: E.matmul(bA[:, 0:128], kT[:, c0:c0 + 128], qT[:, c0:c0 + 128],
                                              start=True, stop=True),
                 reads=[R("kT"), R("qT")], writes=[R(nA)])
            yield
            P.op("vector", lambda E: E.scalar_tensor_tensor(atm[h][:], bA[:, 0:128], 1.0, triu_sb[:],
                                                            ALU.mult, ALU.mult),
                 reads=[R(nA), R("triu")], writes=[R(f"atm{h}")])
            yield
            P.op("tensor", lambda E: E.matmul(bB[:, 0:128], qT[:, c0:c0 + 128], Sk_dm[h][:],
                                              start=True, stop=False),
                 reads=[R("qT"), R(ndm)], writes=[R(nB)])
            yield
            P.op("tensor", lambda E: E.matmul(bB[:, 0:128], atm[h][:], u_tm[:, c0:c0 + 128],
                                              start=False, stop=True),
                 reads=[R(f"atm{h}"), R("u_tm")], writes=[R(nB)])
            yield
            P.op("vector", lambda E: E.scalar_tensor_tensor(s_sb[h][:], bB[:, 0:128], SCALE, gsl,
                                                            ALU.mult, ALU.mult),
                 reads=[R(nB), R("gres")], writes=[R(f"s{h}")])
            yield
            P.op("vector", lambda E: E.reduce_max(nmax[h][:], s_sb[h][:], axis=AX, negate=True),
                 reads=[R(f"s{h}")], writes=[R(f"nmax{h}")])
            yield
            P.op("scalar", lambda E: E.activation(out=e_sb[h][:], in_=s_sb[h][:], func=AF.Exp,
                                                  bias=nmax[h][:, 0:1], accum_out=rsum[h][:]),
                 reads=[R(f"s{h}"), R(f"nmax{h}")], writes=[R(f"e{h}"), R(f"rsum{h}")])
            yield
            P.op("scalar", lambda E: E.activation(out=rinv[h][:], in_=rsum[h][:], func=AF.Ln),
                 reads=[R(f"rsum{h}")], writes=[R(f"rinv{h}")])
            yield
            P.op("scalar", lambda E: E.activation(out=rinv[h][:], in_=rinv[h][:], func=AF.Exp,
                                                  scale=-1.0),
                 reads=[R(f"rinv{h}")], writes=[R(f"rinv{h}")])
            yield
            P.op("vector", lambda E: E.scalar_tensor_tensor(at_c[h][:], e_sb[h][:], rinv[h][:, 0:1],
                                                            gsl, ALU.mult, ALU.mult),
                 reads=[R(f"e{h}"), R(f"rinv{h}"), R("gres")], writes=[R(f"atc{h}")])
            yield
            P.op("tensor", lambda E: E.transpose(bT[:, 0:128], at_c[h][:], id_sb[:]),
                 reads=[R(f"atc{h}"), R("id")], writes=[R(nT)])
            yield
            P.op("scalar", lambda E: E.activation(out=atT[h][:], in_=bT[:, 0:128],
                                                  func=AF.Copy),
                 reads=[R(nT)], writes=[R(f"atT{h}")])
            yield
            P.op("tensor", lambda E: E.matmul(bA[0:64, 128:192], uT[:, c0:c0 + 64], atT[h][:, 0:64],
                                              start=True, stop=True),
                 reads=[R("uT"), R(f"atT{h}")], writes=[R(nA)])
            yield
            P.op("tensor", lambda E: E.matmul(bA[64:128, 192:256], uT[:, c0 + 64:c0 + 128],
                                              atT[h][:, 64:128], start=True, stop=True),
                 reads=[R("uT"), R(f"atT{h}")], writes=[R(nA)])
            yield
            P.op("tensor", lambda E: E.matmul(bA[0:64, 192:256], uT[:, c0:c0 + 64], atT[h][:, 64:128],
                                              start=True, stop=True),
                 reads=[R("uT"), R(f"atT{h}")], writes=[R(nA)])
            yield
            P.op("vector", lambda E: E.tensor_mul(btm[h][0:64, 0:64], bA[0:64, 128:192],
                                                  triu_sb[0:64, 0:64]),
                 reads=[R(nA), R("triu")], writes=[R(f"btm{h}")])
            yield
            P.op("vector", lambda E: E.tensor_mul(btm[h][64:128, 64:128], bA[64:128, 192:256],
                                                  triu_sb[0:64, 0:64]),
                 reads=[R(nA), R("triu")], writes=[R(f"btm{h}")])
            yield
            P.op("scalar", lambda E: E.activation(out=btm[h][0:64, 64:128], in_=bA[0:64, 192:256],
                                                  func=AF.Copy),
                 reads=[R(nA)], writes=[R(f"btm{h}")])
            yield
            P.op("tensor", lambda E: E.matmul(bB[:, 128:256], SvT_sc[h][:], atT[h][:],
                                              start=True, stop=False),
                 reads=[R(nsc), R(f"atT{h}")], writes=[R(nB)])
            yield
            P.op("tensor", lambda E: E.matmul(bB[:, 128:256], v_tm[:, c0:c0 + 128], btm[h][:],
                                              start=False, stop=True),
                 reads=[R("v_tm"), R(f"btm{h}")], writes=[R(nB)])
            yield
            P.op("scalar", lambda E: E.activation(out=osq[h][:], in_=bB[:, 128:256], func=AF.Square),
                 reads=[R(nB)], writes=[R(f"osq{h}")])
            yield
            P.op("tensor", lambda E: E.matmul(bB[0:1, 384:512], ones_sb[:], osq[h][:],
                                              start=True, stop=True),
                 reads=[R(f"osq{h}"), R("ones")], writes=[R(nB)])
            yield
            P.op("scalar", lambda E: E.activation(out=orow[h][:], in_=bB[0:1, 384:512], func=AF.Ln,
                                                  scale=1.0 / 128.0, bias=biasc[0:1, 1:2]),
                 reads=[R(nB), R("biasc")], writes=[R(f"orow{h}")])
            yield
            P.op("scalar", lambda E: E.activation(out=orow_bf[h][:], in_=orow[h][:], func=AF.Exp,
                                                  scale=-0.5),
                 reads=[R(f"orow{h}")], writes=[R(f"orowbf{h}")])
            yield
            P.op("tensor", lambda E: E.matmul(bB[:, 256:384], onesr_sb[:], orow_bf[h][:],
                                              start=True, stop=True),
                 reads=[R(f"orowbf{h}"), R("onesr")], writes=[R(nB)])
            yield
            P.op("vector", lambda E: E.tensor_mul(o1[h][:], bB[:, 128:256], Gt[:, c0:c0 + 128]),
                 reads=[R(nB), R("Gt")], writes=[R(f"o1{h}")])
            yield
            P.op("vector", lambda E: E.tensor_mul(o_seg[:, c0:c0 + 128], o1[h][:], bB[:, 256:384]),
                 reads=[R(f"o1{h}"), R(nB)], writes=[R("o_seg")])
            yield
            P.op("tensor", lambda E: E.matmul(bA[:, 256:384], u_tm[:, c0:c0 + 128],
                                              k_tm[:, c0:c0 + 128], start=True, stop=True),
                 reads=[R("u_tm"), R("k_tm")], writes=[R(nA)])
            yield
            P.op("tensor", lambda E: E.matmul(bA[:, 384:512], u_tm[:, c0:c0 + 128],
                                              v_tm[:, c0:c0 + 128], start=True, stop=True),
                 reads=[R("u_tm"), R("v_tm")], writes=[R(nA)])
            yield
            src, dst = pp, 1 - pp
            P.op("vector", lambda E: E.tensor_scalar(tsk[h][:], bA[:, 256:384], elmr[:, ci:ci + 1],
                                                     None, ALU.mult),
                 reads=[R(nA), R("elmr")], writes=[R(f"tsk{h}")])
            yield
            P.op("vector", lambda E, src=src, dst=dst: E.scalar_tensor_tensor(
                SkT[h][:, dst * 128:dst * 128 + 128], SkT[h][:, src * 128:src * 128 + 128],
                glast[:, ci:ci + 1], tsk[h][:], ALU.mult, ALU.add),
                 reads=[R("glast"), R(f"tsk{h}")], writes=[R(nsk)])
            yield
            P.op("vector", lambda E: E.tensor_scalar(tsv[h][:], bA[:, 384:512], elmr[:, ci:ci + 1],
                                                     None, ALU.mult),
                 reads=[R(nA), R("elmr")], writes=[R(f"tsv{h}")])
            yield
            P.op("vector", lambda E, src=src, dst=dst: E.scalar_tensor_tensor(
                SvT[h][:, dst * 128:dst * 128 + 128], SvT[h][:, src * 128:src * 128 + 128],
                glast[:, ci:ci + 1], tsv[h][:], ALU.mult, ALU.add),
                 reads=[R("glast"), R(f"tsv{h}")], writes=[R(nsv)])
            yield
            if b + 1 < CPS:
                cin = h * CPS + b + 1
                P.op("vector", lambda E, dst=dst, cin=cin: E.tensor_scalar(
                    skstage[h][:], SkT[h][:, dst * 128:dst * 128 + 128], eref[:, cin:cin + 1],
                    None, ALU.mult),
                     reads=[R(nsk), R("eref")], writes=[R(f"skst{h}")])
                yield
                P.op("tensor", lambda E: E.transpose(bT[:, 128:256], skstage[h][:], id_sb[:]),
                     reads=[R(f"skst{h}"), R("id")], writes=[R(nT)])
                yield
                P.op("scalar", lambda E: E.activation(out=Sk_dm[h][:], in_=bT[:, 128:256],
                                                      func=AF.Copy),
                     reads=[R(nT)], writes=[R(ndm)])
                yield
                P.op("vector", lambda E, dst=dst, cin=cin: E.tensor_scalar(
                    SvT_sc[h][:], SvT[h][:, dst * 128:dst * 128 + 128], eref[:, cin:cin + 1],
                    None, ALU.mult),
                     reads=[R(nsv), R("eref")], writes=[R(nsc)])
                yield

        def rescale_state(s):
            for h in range(2):
                pp = (s * CPS) % 2
                ci = h * CPS + 0
                nsk, nsv = f"SkT{h}", f"SvT{h}"
                ndm, nsc = f"Skdm{h}", f"Svsc{h}"
                bT, nT = bkT[h], f"bkT{h}"
                P.op("vector", lambda E, h=h, pp=pp, ci=ci: E.tensor_scalar(
                    skstage[h][:], SkT[h][:, pp * 128:pp * 128 + 128], eref[:, ci:ci + 1],
                    None, ALU.mult),
                     reads=[R(nsk), R("eref")], writes=[R(f"skst{h}")])
                P.op("tensor", lambda E, h=h, bT=bT: E.transpose(bT[:, 128:256], skstage[h][:],
                                                                 id_sb[:]),
                     reads=[R(f"skst{h}"), R("id")], writes=[R(nT)])
                P.op("scalar", lambda E, h=h, bT=bT: E.activation(out=Sk_dm[h][:], in_=bT[:, 128:256],
                                                                  func=AF.Copy),
                     reads=[R(nT)], writes=[R(ndm)])
                P.op("vector", lambda E, h=h, pp=pp, ci=ci: E.tensor_scalar(
                    SvT_sc[h][:], SvT[h][:, pp * 128:pp * 128 + 128], eref[:, ci:ci + 1],
                    None, ALU.mult),
                     reads=[R(nsv), R("eref")], writes=[R(nsc)])

        id_f32 = np.eye(128, dtype=np.float32)
        idf_h = nc.inline_tensor(id_f32, name="a_identf")
        id_f32_sb = sb("id_f32_sb", [128, 128], F32)
        rx["idf"] = Res()
        P.dma("gpsimd", id_f32_sb[:], idf_h.ap()[:, :], writes=[R("idf")])
        skstage = [sb(f"skstage{h}", [128, 128], BF16) for h in range(2)]
        for _h in range(2):
            rx[f"skst{_h}"] = Res()
        biasc = sb("biasc", [128, 4], F32)
        rx["biasc"] = Res()
        P.op("vector", lambda E: E.memset(biasc[:, 0:1], 1e-6), writes=[R("biasc")])
        P.op("vector", lambda E: E.memset(biasc[:, 1:2], 1e-5), writes=[R("biasc")])
        P.op("vector", lambda E: E.memset(biasc[:, 2:3], 1.0), writes=[R("biasc")])

        seg_hook = (ctx or {}).get("seg_hook")
        q_tick = (ctx or {}).get("q_tick")
        for s in range(NSEG):
            for k in range(16):
                P.op("sync",
                     lambda E, s=s, k=k: E.dma_start_transpose(
                         xt[:, k * SEG:(k + 1) * SEG],
                         xbf[s * SEG:(s + 1) * SEG, k * 128:(k + 1) * 128]),
                     reads=[R("xbf")], writes=[R("xt")],
                     extra=[xbf_tok[s]], dma=True)
            if DBG_PROJ:
                for t in range(SEG // 512):
                    proj_tile(s, t, t * 512, 512)
            if DBG_SCAN:
                rescale_state(s)
                for b in range(CPS):
                    gens = [scan_chunk(s, 0, b), scan_chunk(s, 1, b)]
                    alive = [True, True]
                    rnd = 0
                    while alive[0] or alive[1]:
                        for i in (0, 1):
                            if alive[i]:
                                try:
                                    next(gens[i])
                                except StopIteration:
                                    alive[i] = False
                        rnd += 1
                        if q_tick is not None and rnd % 16 == 0:
                            q_tick()
            for cc in range(2):
                tok = P.dma("sync", o_dst(s, cc),
                            o_seg[:, cc * SEG:(cc + 1) * SEG], reads=[R("o_seg")])
                out_toks.append(tok)

        if ctx is None:
            P.op("sync", lambda E: E.nop(), extra=out_toks)
            P.finalize(st)
        else:
            if seg_hook is not None:
                seg_hook(NSEG)
            ctx["a_out_toks"] = out_toks
            P.barrier()
    return nc



def build_Q(ctx=None):
    g = _q_gen(ctx)
    nc = next(g)
    if ctx is None or not ctx.get("q_defer"):
        for _ in g:
            pass
    else:
        ctx["q_steps"] = g
    return nc


def _q_gen(ctx=None):
    if ctx is None:
        nc = bass.Bass("TRN2", target_bir_lowering=False)
        wgu_q = nc.dram_tensor("wgu_q", [256, 16384], BF16, kind="ExternalOutput").ap()
        wpr_q = nc.dram_tensor("wpr_q", [1024, 2048], BF16, kind="ExternalOutput").ap()
        ngu, npr = 256, 1024
    else:
        nc = ctx["nc"]
        wgu_q = ctx["wgu_q"]
        wpr_q = ctx["wpr_q"]
        ngu, npr = ctx.get("q_rows", (256, 1024))
    wgu_sh = nc.dram_tensor("wgu_sh", [ngu, 16384], F32, kind="ExternalInput").ap()
    wpr_sh = nc.dram_tensor("wpr_sh", [npr, 2048], F32, kind="ExternalInput").ap()
    ones_h = nc.inline_tensor(np.ones((128, 1), dtype=bfnp), name="q_ones_col")
    sel0 = np.zeros((1, 128), dtype=bfnp); sel0[0, :64] = 1
    sel1 = np.zeros((1, 128), dtype=bfnp); sel1[0, 64:] = 1
    sel0_h = nc.inline_tensor(sel0, name="q_sel0")
    sel1_h = nc.inline_tensor(sel1, name="q_sel1")
    sel0f_h = nc.inline_tensor(sel0.astype(np.float32), name="q_sel0f")
    sel1f_h = nc.inline_tensor(sel1.astype(np.float32), name="q_sel1f")

    with contextlib.ExitStack() as st:
        sb = lambda name, shape, dt: st.enter_context(nc.sbuf_tensor("q_" + name, shape, dt))
        w_bf = sb("w_bf", [128, 4096], BF16)
        q_bf = sb("q_bf", [128, 4096], BF16)
        aw = [sb(f"aw{i}", [128, 512], BF16) for i in range(2)]
        r_bf = [sb(f"r_bf{i}", [128, 512], BF16) for i in range(2)]
        mpos = [sb(f"mpos{i}", [128, 512], BF16) for i in range(2)]
        mneg = [sb(f"mneg{i}", [128, 512], BF16) for i in range(2)]
        s0r = [sb(f"s0r{i}", [1, 512], BF16) for i in range(2)]
        s1r = [sb(f"s1r{i}", [1, 512], BF16) for i in range(2)]
        si0 = [sb(f"si0{i}", [1, 512], F32) for i in range(2)]
        si1 = [sb(f"si1{i}", [1, 512], F32) for i in range(2)]
        ones_sb = sb("ones_sb", [128, 1], BF16)
        sel0_sb = sb("sel0_sb", [1, 128], BF16)
        sel1_sb = sb("sel1_sb", [1, 128], BF16)
        sel0f_sb = sb("sel0f_sb", [1, 128], F32)
        sel1f_sb = sb("sel1f_sb", [1, 128], F32)
        pSS = st.enter_context(nc.psum_tensor("q_pSS", [128, 512], F32))
        prr = st.enter_context(nc.psum_tensor("q_prr", [1, 512], F32))

        P = Prog(nc) if ctx is None else ctx["P"]
        rx = {}
        for n in ("wbf", "qbf", "ones", "sel0", "sel1", "sel0f", "sel1f"):
            rx[n] = Res()
        for i in range(2):
            for n in ("aw", "r", "mp", "mn", "s0", "s1", "si0", "si1"):
                rx[f"{n}{i}"] = Res()
        rx["pSS"] = Res(excl=True)
        rx["prr"] = Res(excl=True)
        R = lambda n: rx[n]
        P.dma("gpsimd", ones_sb[:], ones_h.ap()[:, :], writes=[R("ones")])
        P.dma("gpsimd", sel0_sb[:], sel0_h.ap()[:, :], writes=[R("sel0")])
        P.dma("gpsimd", sel1_sb[:], sel1_h.ap()[:, :], writes=[R("sel1")])
        P.dma("gpsimd", sel0f_sb[:], sel0f_h.ap()[:, :], writes=[R("sel0f")])
        P.dma("gpsimd", sel1f_sb[:], sel1f_h.ap()[:, :], writes=[R("sel1f")])
        out_toks = []

        def quant_rows(src, dst, nrow, ncol):
            npiece = max(1, ncol // 4096)
            pw = ncol // npiece
            for pc0 in range(nrow // 128 * npiece):
                pc, pcc = divmod(pc0, npiece)
                cs = slice(pcc * pw, (pcc + 1) * pw)
                P.dma("gpsimd", w_bf[:, :pw], src[pc * 128:(pc + 1) * 128, cs],
                      writes=[R("wbf")])
                for isub, f0 in enumerate(range(0, pw, 512)):
                    fs = slice(f0, f0 + 512)
                    i = isub % 2
                    awi, ri, mpi, mni = aw[i], r_bf[i], mpos[i], mneg[i]
                    s0i, s1i, si0i, si1i = s0r[i], s1r[i], si0[i], si1[i]
                    sfx = str(i)
                    P.op("scalar", lambda E, fs=fs, awi=awi: E.activation(
                        out=awi[:], in_=w_bf[:, fs], func=AF.Abs),
                         reads=[R("wbf")], writes=[R("aw" + sfx)])
                    P.op("tensor", lambda E, awi=awi: E.matmul(
                        prr[:], ones_sb[0:64, :], awi[0:64, :], start=True, stop=True),
                         reads=[R("aw" + sfx), R("ones")], writes=[R("prr")])
                    P.op("vector", lambda E, s0i=s0i: E.tensor_scalar(
                        s0i[:], prr[:], 1.0 / 64.0, None, ALU.mult),
                         reads=[R("prr")], writes=[R("s0" + sfx)])
                    P.op("tensor", lambda E, awi=awi: E.matmul(
                        prr[:], ones_sb[64:128, :], awi[64:128, :], start=True, stop=True),
                         reads=[R("aw" + sfx), R("ones")], writes=[R("prr")])
                    P.op("vector", lambda E, s1i=s1i: E.tensor_scalar(
                        s1i[:], prr[:], 1.0 / 64.0, None, ALU.mult),
                         reads=[R("prr")], writes=[R("s1" + sfx)])
                    P.op("vector", lambda E, si0i=si0i, s0i=s0i: E.reciprocal(si0i[:], s0i[:]),
                         reads=[R("s0" + sfx)], writes=[R("si0" + sfx)])
                    P.op("vector", lambda E, si1i=si1i, s1i=s1i: E.reciprocal(si1i[:], s1i[:]),
                         reads=[R("s1" + sfx)], writes=[R("si1" + sfx)])
                    P.op("tensor", lambda E, si0i=si0i: E.matmul(
                        pSS[:], sel0f_sb[:], si0i[:], start=True, stop=False),
                         reads=[R("si0" + sfx), R("sel0f")], writes=[R("pSS")])
                    P.op("tensor", lambda E, si1i=si1i: E.matmul(
                        pSS[:], sel1f_sb[:], si1i[:], start=False, stop=True),
                         reads=[R("si1" + sfx), R("sel1f")], writes=[R("pSS")])
                    P.op("vector", lambda E, fs=fs, ri=ri: E.tensor_mul(
                        ri[:], w_bf[:, fs], pSS[:]),
                         reads=[R("wbf"), R("pSS")], writes=[R("r" + sfx)])
                    P.op("tensor", lambda E, s0i=s0i: E.matmul(
                        pSS[:], sel0_sb[:], s0i[:], start=True, stop=False),
                         reads=[R("s0" + sfx), R("sel0")], writes=[R("pSS")])
                    P.op("tensor", lambda E, s1i=s1i: E.matmul(
                        pSS[:], sel1_sb[:], s1i[:], start=False, stop=True),
                         reads=[R("s1" + sfx), R("sel1")], writes=[R("pSS")])
                    P.op("vector", lambda E, mpi=mpi, ri=ri: E.tensor_single_scalar(
                        mpi[:], ri[:], 0.5, ALU.is_gt),
                         reads=[R("r" + sfx)], writes=[R("mp" + sfx)])
                    P.op("vector", lambda E, mni=mni, ri=ri: E.tensor_single_scalar(
                        mni[:], ri[:], -0.5, ALU.is_lt),
                         reads=[R("r" + sfx)], writes=[R("mn" + sfx)])
                    P.op("vector", lambda E, mpi=mpi, mni=mni: E.tensor_sub(
                        mpi[:], mpi[:], mni[:]),
                         reads=[R("mn" + sfx)], writes=[R("mp" + sfx)])
                    P.op("vector", lambda E, fs=fs, mpi=mpi: E.tensor_mul(
                        q_bf[:, fs], mpi[:], pSS[:]),
                         reads=[R("mp" + sfx), R("pSS")], writes=[R("qbf")])
                    yield
                tok = P.dma("sync", dst[pc * 128:(pc + 1) * 128, cs], q_bf[:, :pw],
                            reads=[R("qbf")])
                out_toks.append(tok)
                yield

        yield nc
        yield from quant_rows(wgu_sh, wgu_q, ngu, 16384)
        yield from quant_rows(wpr_sh, wpr_q, npr, 2048)
        if ctx is None:
            P.op("sync", lambda E: E.nop(), extra=out_toks)
            P.finalize(st)
        else:
            ctx["q_out_toks"] = out_toks
            yield "done"


LNEPS = 1e-5
RMSEPS = float(np.finfo(np.float32).eps)


def build_B(ctx=None):
    if ctx is None:
        nc = bass.Bass("TRN2", target_bir_lowering=False)
        oT_src = nc.dram_tensor("oT", [128, 16 * 1024], BF16, kind="ExternalInput").ap()[:, :]
        wguq_Pt = nc.dram_tensor("wguq_P", [128, 128, 2048], BF16, kind="ExternalInput").ap()
        wprq_Pt = nc.dram_tensor("wprq_P", [16, 128, 8192], BF16, kind="ExternalInput").ap()
        gu_src = lambda ocb: wguq_Pt[ocb, :, :]
        pr_src = lambda oc, half: wprq_Pt[oc, :, half * 4096:(half + 1) * 4096]
        cgate = []
        ctx = None
    else:
        nc = ctx["nc"]
        oT_src = ctx["oT_src"]
        gu_src = ctx["gu_src"]
        pr_src = ctx["pr_src"]
        cgate = ctx["cgate"]
    pre_oT = ctx.get("pre_oT") if ctx else None
    pre_gu = ctx.get("pre_gu") if ctx else None
    x_sl = nc.dram_tensor("x_sl", [1024, HID], F32, kind="ExternalInput").ap()
    wo_r = nc.dram_tensor("wo_r", [16, 128, 2048], F32, kind="ExternalInput").ap()
    rg_r = nc.dram_tensor("rg_r", [128, 128], F32, kind="ExternalInput").ap()
    egu_g = nc.dram_tensor("egu_g", [8, 8, 128, 2048], F32, kind="ExternalInput").ap()
    egu_u = nc.dram_tensor("egu_u", [8, 8, 128, 2048], F32, kind="ExternalInput").ap()
    edn_r = nc.dram_tensor("edn_r", [16, 128, 8192], F32, kind="ExternalInput").ap()
    bt_r = nc.dram_tensor("bt_r", [4, 128, 4096], F32, kind="ExternalInput").ap()
    ct_r = nc.dram_tensor("ct_r", [4, 128, 4096], F32, kind="ExternalInput").ap()
    dvec = nc.dram_tensor("dvec", [128, 16], F32, kind="ExternalInput").ap()
    out = nc.dram_tensor("out", [1024, HID], F32, kind="ExternalOutput").ap()

    identf_h = nc.inline_tensor(np.eye(128, dtype=np.float32), name="b_identf")
    id8f_h = nc.inline_tensor(np.eye(8, dtype=np.float32), name="b_id8f")
    ones_h = nc.inline_tensor(np.ones((128, 1), dtype=bfnp), name="b_ones_col")
    onesrf_h = nc.inline_tensor(np.ones((1, 128), dtype=np.float32), name="b_ones_rowf")
    sel = np.zeros((8, 8 * 128), dtype=bfnp)
    for e in range(8):
        sel[e, e * 128:(e + 1) * 128] = 1
    sel_h = nc.inline_tensor(sel, name="b_sel_e")

    with contextlib.ExitStack() as st:
        sb = lambda name, shape, dt: st.enter_context(nc.sbuf_tensor("b_" + name, shape, dt))
        resid = sb("resid", [128, 16 * 1024], F32)
        xbf_s = sb("xbf_s", [128, 16 * 1024], BF16)
        hbuf = sb("hbuf", [128, 64 * 512], BF16)
        wsA = sb("wsA", [128, 4096], BF16)
        wsB = sb("wsB", [128, 4096], BF16)
        stb = hbuf
        idf_sb = sb("idf_sb", [128, 128], F32)
        id8f_sb = sb("id8f_sb", [8, 8], F32)
        ones_sb = sb("ones_sb", [128, 1], BF16)
        onesrf_sb = sb("onesrf_sb", [1, 128], F32)
        sel_sb = sb("sel_sb", [8, 8 * 128], BF16)
        rg_sb = sb("rg_sb", [128, 128], BF16)
        dv_sb = sb("dv_sb", [128, 16], F32)
        biasc = sb("biasc", [128, 4], F32)
        sq_bf = sb("sq_bf", [128, 512], BF16)
        sq_bf2 = sb("sq_bf2", [128, 512], BF16)
        xtile = sb("xtile", [128, 512], F32)
        lgf = sb("lgf", [8, 512], F32)
        lg_tm = sb("lg_tm", [128, 32], F32)
        m1c = sb("m1c", [128, 1], F32)
        m2c = sb("m2c", [128, 1], F32)
        eq1 = sb("eq1", [128, 8], F32)
        eq2 = sb("eq2", [128, 8], F32)
        l2m = sb("l2m", [128, 8], F32)
        d12 = sb("d12", [128, 1], F32)
        s12 = sb("s12", [128, 1], F32)
        s21 = sb("s21", [128, 1], F32)
        we_tm = sb("we_tm", [128, 32], F32)
        weT = sb("weT", [8, 2 * 512], BF16)
        webc_sb = sb("webc_sb", [128, 512], BF16)
        sgt = sb("sgt", [128, 512], BF16)
        h1t = sb("h1t", [128, 512], BF16)
        rowa = sb("rowa", [1, 512], F32)
        rowb = sb("rowb", [1, 512], F32)
        rowc = sb("rowc", [1, 512], F32)
        rowd = sb("rowd", [1, 512], F32)
        p0 = st.enter_context(nc.psum_tensor("b_p0", [128, 512], F32))
        p1 = st.enter_context(nc.psum_tensor("b_p1", [128, 512], F32))
        p2 = st.enter_context(nc.psum_tensor("b_p2", [128, 512], F32))
        p3 = st.enter_context(nc.psum_tensor("b_p3", [128, 512], F32))
        pTf = st.enter_context(nc.psum_tensor("b_pTf", [128, 128], F32))
        pS1 = st.enter_context(nc.psum_tensor("b_pS1", [1, 512], F32))
        pS2 = st.enter_context(nc.psum_tensor("b_pS2", [1, 512], F32))

        P = Prog(nc) if ctx is None else ctx["P"]
        rx = {n: Res() for n in (
            "resid", "xbf", "hbuf", "stb", "idf", "id8f", "ones", "onesrf",
            "sel", "rg", "dv", "biasc", "sq", "sq2", "xtile", "lgf", "lgtm", "m1",
            "m2", "eq1", "eq2", "l2m", "d12", "s12", "s21", "wetm", "weT",
            "sgt", "h1t", "rowa", "rowb", "rowc", "rowd", "webc")}
        for a in "AB":
            for q in range(2):
                rx[f"ws{a}{q}"] = Res()
        for n in ("p0", "p1", "p2", "p3", "pTf", "pS1", "pS2"):
            rx[n] = Res(excl=True)
        R = lambda n: rx[n]
        AR = {"A": wsA, "B": wsB}

        def qslice(a, q):
            return AR[a][:, q * 2048:(q + 1) * 2048]

        def load_quarter(a, q, src, eng="gpsimd"):
            return P.dma(eng, qslice(a, q), src, writes=[R(f"ws{a}{q}")])

        def load_arena(a, src, eng="gpsimd"):
            return P.dma(eng, AR[a][:, :], src,
                         writes=[R(f"ws{a}{q}") for q in range(2)])

        def arena_res(a):
            return [R(f"ws{a}{q}") for q in range(2)]

        def gu_load(a, q, ocb):
            if ctx is None:
                P.dma("sync", qslice(a, q), gu_src(ocb), writes=[R(f"ws{a}{q}")])
            else:
                dview = qslice(a, q).rearrange("p (k c) -> p k c", k=16, c=128)
                P.dma("sync", dview, gu_src(ocb), writes=[R(f"ws{a}{q}")],
                      extra=list(cgate))

        def pr_load(a, oc, half):
            if ctx is None:
                P.dma("sync", AR[a][:, :], pr_src(oc, half),
                      writes=[R(f"ws{a}0"), R(f"ws{a}1")])
            else:
                dview = AR[a][:, :].rearrange("p (hc c) -> p hc c", hc=32, c=128)
                P.dma("sync", dview, pr_src(oc, half),
                      writes=[R(f"ws{a}0"), R(f"ws{a}1")], extra=list(cgate))

        P.dma("gpsimd", idf_sb[:], identf_h.ap()[:, :], writes=[R("idf")])
        P.dma("gpsimd", id8f_sb[:], id8f_h.ap()[:, :], writes=[R("id8f")])
        P.dma("gpsimd", ones_sb[:], ones_h.ap()[:, :], writes=[R("ones")])
        P.dma("gpsimd", onesrf_sb[:], onesrf_h.ap()[:, :], writes=[R("onesrf")])
        P.dma("gpsimd", sel_sb[:], sel_h.ap()[:, :], writes=[R("sel")])
        P.dma("gpsimd", rg_sb[:], rg_r[:, :], writes=[R("rg")])
        P.dma("gpsimd", dv_sb[:], dvec[:, :], writes=[R("dv")])
        P.op("vector", lambda E: E.memset(biasc[:, 0:1], LNEPS), writes=[R("biasc")])
        P.op("vector", lambda E: E.memset(biasc[:, 1:2], RMSEPS), writes=[R("biasc")])

        for tt in range(8):
            for cs in range(4):
                P.dma("gpsimd", xtile[:],
                      x_sl[tt * 128:(tt + 1) * 128, cs * 512:(cs + 1) * 512],
                      writes=[R("xtile")])
                for k4 in range(4):
                    k = cs * 4 + k4
                    P.op("tensor", lambda E, k4=k4: E.transpose(
                        pTf[:], xtile[:, k4 * 128:(k4 + 1) * 128], idf_sb[:]),
                         reads=[R("xtile"), R("idf")], writes=[R("pTf")])
                    P.op("vector", lambda E, k=k, tt=tt: E.tensor_copy(
                        resid[:, k * 1024 + tt * 128:k * 1024 + tt * 128 + 128], pTf[:]),
                         reads=[R("pTf")], writes=[R("resid")])
        if pre_oT is not None:
            pre_oT(P)
        P.dma("gpsimd", hbuf[:, 0:16 * 1024], oT_src, writes=[R("hbuf")],
              extra=list(cgate))

        def ln_stage():
            for t in range(2):
                for k in range(16):
                    c0 = k * 1024 + t * 512
                    sq, sqn = (sq_bf, "sq") if k % 2 == 0 else (sq_bf2, "sq2")
                    P.op("vector", lambda E, c0=c0, sq=sq: E.tensor_copy(sq[:], resid[:, c0:c0 + 512]),
                         reads=[R("resid")], writes=[R(sqn)])
                    P.op("tensor", lambda E, k=k, sq=sq: E.matmul(pS1[:], ones_sb[:], sq[:],
                                                                  start=(k == 0), stop=(k == 15),
                                                                  skip_group_check=True),
                         reads=[R(sqn), R("ones")], writes=[R("pS1")])
                    P.op("scalar", lambda E, sq=sq: E.activation(out=sq[:], in_=sq[:], func=AF.Square),
                         reads=[R(sqn)], writes=[R(sqn)])
                    P.op("tensor", lambda E, k=k, sq=sq: E.matmul(pS2[:], ones_sb[:], sq[:],
                                                                  start=(k == 0), stop=(k == 15),
                                                                  skip_group_check=True),
                         reads=[R(sqn), R("ones")], writes=[R("pS2")])
                P.op("vector", lambda E: E.tensor_scalar(rowa[:], pS1[:], 1.0 / 2048.0, None, ALU.mult),
                     reads=[R("pS1")], writes=[R("rowa")])
                P.op("scalar", lambda E: E.activation(out=rowb[:], in_=rowa[:], func=AF.Square),
                     reads=[R("rowa")], writes=[R("rowb")])
                P.op("vector", lambda E: E.scalar_tensor_tensor(rowc[:], pS2[:], 1.0 / 2048.0,
                                                                rowb[:], ALU.mult, ALU.subtract),
                     reads=[R("pS2"), R("rowb")], writes=[R("rowc")])
                P.op("scalar", lambda E: E.activation(out=rowc[:], in_=rowc[:], func=AF.Ln,
                                                      bias=biasc[0:1, 0:1]),
                     reads=[R("biasc")], writes=[R("rowc")])
                P.op("scalar", lambda E: E.activation(out=rowd[:], in_=rowc[:], func=AF.Exp,
                                                      scale=-0.5),
                     reads=[R("rowc")], writes=[R("rowd")])
                P.op("vector", lambda E: E.tensor_mul(rowb[:], rowa[:], rowd[:]),
                     reads=[R("rowa"), R("rowd")], writes=[R("rowb")])
                P.op("vector", lambda E: E.tensor_scalar(rowb[:], rowb[:], -1.0, None, ALU.mult),
                     writes=[R("rowb")])
                P.op("tensor", lambda E: E.matmul(p2[:], onesrf_sb[:], rowd[:],
                                                  start=True, stop=True),
                     reads=[R("rowd"), R("onesrf")], writes=[R("p2")])
                P.op("tensor", lambda E: E.matmul(p3[:], onesrf_sb[:], rowb[:],
                                                  start=True, stop=True),
                     reads=[R("rowb"), R("onesrf")], writes=[R("p3")])
                for k in range(16):
                    c0 = k * 1024 + t * 512
                    P.op("vector", lambda E, c0=c0: E.tensor_mul(resid[:, c0:c0 + 512],
                                                                 resid[:, c0:c0 + 512], p2[:]),
                         reads=[R("p2")], writes=[R("resid")])
                    P.op("vector", lambda E, c0=c0: E.tensor_add(resid[:, c0:c0 + 512],
                                                                 resid[:, c0:c0 + 512], p3[:]),
                         reads=[R("p3")], writes=[R("resid")])
                    P.op("gpsimd", lambda E, c0=c0: E.tensor_copy(xbf_s[:, c0:c0 + 512],
                                                                  resid[:, c0:c0 + 512]),
                         reads=[R("resid")], writes=[R("xbf")])

        for oc in range(16):
            a, q = ("A", "B")[(oc // 2) % 2], oc % 2
            load_quarter(a, q, wo_r[oc, :, :])
            for t in range(2):
                pb, pn = (p0, "p0") if (oc * 2 + t) % 2 == 0 else (p1, "p1")
                for k in range(16):
                    P.op("tensor", lambda E, a=a, q=q, k=k, t=t, pb=pb: E.matmul(
                        pb[:], qslice(a, q)[:, k * 128:(k + 1) * 128],
                        hbuf[:, k * 1024 + t * 512:k * 1024 + t * 512 + 512],
                        start=(k == 0), stop=(k == 15)),
                         reads=[R(f"ws{a}{q}"), R("hbuf")], writes=[R(pn)])
                c0 = oc * 1024 + t * 512
                P.op("vector", lambda E, c0=c0, pb=pb: E.tensor_add(resid[:, c0:c0 + 512],
                                                                    resid[:, c0:c0 + 512], pb[:]),
                     reads=[R(pn)], writes=[R("resid")])
        ln_stage()

        for t in range(2):
            for k in range(16):
                P.op("tensor", lambda E, k=k, t=t: E.matmul(
                    p0[0:8, :], rg_sb[:, k * 8:(k + 1) * 8],
                    xbf_s[:, k * 1024 + t * 512:k * 1024 + t * 512 + 512],
                    start=(k == 0), stop=(k == 15)),
                     reads=[R("rg"), R("xbf")], writes=[R("p0")])
            P.op("vector", lambda E: E.tensor_copy(lgf[:], p0[0:8, :]),
                 reads=[R("p0")], writes=[R("lgf")])
            for b in range(4):
                P.op("tensor", lambda E, b=b: E.transpose(pTf[0:128, 0:8],
                                                          lgf[:, b * 128:(b + 1) * 128],
                                                          id8f_sb[:]),
                     reads=[R("lgf"), R("id8f")], writes=[R("pTf")])
                P.op("vector", lambda E, b=b: E.tensor_copy(lg_tm[:, b * 8:(b + 1) * 8],
                                                            pTf[0:128, 0:8]),
                     reads=[R("pTf")], writes=[R("lgtm")])
            for b in range(4):
                P.op("vector", lambda E, b=b: E.reduce_max(m1c[:], lg_tm[:, b * 8:(b + 1) * 8],
                                                           axis=AX),
                     reads=[R("lgtm")], writes=[R("m1")])
                P.op("vector", lambda E, b=b: E.tensor_scalar(eq1[:], lg_tm[:, b * 8:(b + 1) * 8],
                                                              m1c[:, 0:1], None, ALU.is_equal),
                     reads=[R("lgtm"), R("m1")], writes=[R("eq1")])
                P.op("vector", lambda E, b=b: E.scalar_tensor_tensor(
                    l2m[:], eq1[:], -1e9, lg_tm[:, b * 8:(b + 1) * 8], ALU.mult, ALU.add),
                     reads=[R("eq1"), R("lgtm")], writes=[R("l2m")])
                P.op("vector", lambda E: E.reduce_max(m2c[:], l2m[:], axis=AX),
                     reads=[R("l2m")], writes=[R("m2")])
                P.op("vector", lambda E: E.tensor_scalar(eq2[:], l2m[:], m2c[:, 0:1], None,
                                                         ALU.is_equal),
                     reads=[R("l2m"), R("m2")], writes=[R("eq2")])
                P.op("vector", lambda E: E.tensor_sub(d12[:], m1c[:], m2c[:]),
                     reads=[R("m1"), R("m2")], writes=[R("d12")])
                P.op("scalar", lambda E: E.activation(out=s12[:], in_=d12[:], func=AF.Sigmoid),
                     reads=[R("d12")], writes=[R("s12")])
                P.op("scalar", lambda E: E.activation(out=s21[:], in_=d12[:], func=AF.Sigmoid,
                                                      scale=-1.0),
                     reads=[R("d12")], writes=[R("s21")])
                P.op("vector", lambda E: E.tensor_scalar(eq1[:], eq1[:], s12[:, 0:1], None, ALU.mult),
                     reads=[R("s12")], writes=[R("eq1")])
                P.op("vector", lambda E, b=b: E.scalar_tensor_tensor(
                    we_tm[:, b * 8:(b + 1) * 8], eq2[:], s21[:, 0:1], eq1[:], ALU.mult, ALU.add),
                     reads=[R("eq2"), R("s21"), R("eq1")], writes=[R("wetm")])
            for b in range(4):
                P.op("tensor", lambda E, b=b: E.transpose(pTf[0:8, 0:128],
                                                          we_tm[:, b * 8:(b + 1) * 8], idf_sb[:]),
                     reads=[R("wetm"), R("idf")], writes=[R("pTf")])
                P.op("vector", lambda E, b=b, t=t: E.tensor_copy(
                    weT[0:8, t * 512 + b * 128:t * 512 + (b + 1) * 128], pTf[0:8, 0:128]),
                     reads=[R("pTf")], writes=[R("weT")])

        for t in range(2):
            for e in range(8):
                P.op("tensor", lambda E, e=e, t=t: E.matmul(
                    p2[:], sel_sb[:, e * 128:(e + 1) * 128], weT[:, t * 512:(t + 1) * 512],
                    start=True, stop=True),
                     reads=[R("sel"), R("weT")], writes=[R("p2")])
                P.op("vector", lambda E: E.tensor_copy(webc_sb[:], p2[:]),
                     reads=[R("p2")], writes=[R("webc")])
                for oc in range(8):
                    a = "A" if (e * 8 + oc) % 2 == 0 else "B"
                    it = e * 8 + oc
                    pg, pgn = (p0, "p0") if it % 2 == 0 else (p2, "p2")
                    pu, pun = (p1, "p1") if it % 2 == 0 else (p3, "p3")
                    load_quarter(a, 0, egu_g[e, oc, :, :])
                    load_quarter(a, 1, egu_u[e, oc, :, :])
                    for k in range(16):
                        P.op("tensor", lambda E, a=a, k=k, t=t, pg=pg: E.matmul(
                            pg[:], qslice(a, 0)[:, k * 128:(k + 1) * 128],
                            xbf_s[:, k * 1024 + t * 512:k * 1024 + t * 512 + 512],
                            start=(k == 0), stop=(k == 15)),
                             reads=[R(f"ws{a}0"), R("xbf")], writes=[R(pgn)])
                    for k in range(16):
                        P.op("tensor", lambda E, a=a, k=k, t=t, pu=pu: E.matmul(
                            pu[:], qslice(a, 1)[:, k * 128:(k + 1) * 128],
                            xbf_s[:, k * 1024 + t * 512:k * 1024 + t * 512 + 512],
                            start=(k == 0), stop=(k == 15)),
                             reads=[R(f"ws{a}1"), R("xbf")], writes=[R(pun)])
                    P.op("scalar", lambda E, pg=pg: E.activation(out=sgt[:], in_=pg[:], func=AF.Silu),
                         reads=[R(pgn)], writes=[R("sgt")])
                    P.op("vector", lambda E: E.tensor_mul(h1t[:], sgt[:], webc_sb[:]),
                         reads=[R("sgt"), R("webc")], writes=[R("h1t")])
                    hc0 = (e * 8 + oc) * 512
                    P.op("vector", lambda E, hc0=hc0, pu=pu: E.tensor_mul(hbuf[:, hc0:hc0 + 512],
                                                                          h1t[:], pu[:]),
                         reads=[R("h1t"), R(pun)], writes=[R("hbuf")])
            for oc in range(16):
                load_arena("A", edn_r[oc, :, 0:4096])
                load_arena("B", edn_r[oc, :, 4096:8192])
                pb, pn = (p0, "p0") if oc % 2 == 0 else (p1, "p1")
                for eh in range(64):
                    a = "A" if eh < 32 else "B"
                    eo = eh % 32
                    P.op("tensor", lambda E, a=a, eo=eo, eh=eh, pb=pb: E.matmul(
                        pb[:], AR[a][:, eo * 128:(eo + 1) * 128],
                        hbuf[:, eh * 512:eh * 512 + 512],
                        start=(eh == 0), stop=(eh == 63)),
                         reads=[R(f"ws{a}{eo // 16}"), R("hbuf")], writes=[R(pn)])
                c0 = oc * 1024 + t * 512
                P.op("vector", lambda E, c0=c0, pb=pb: E.tensor_add(resid[:, c0:c0 + 512],
                                                                    resid[:, c0:c0 + 512], pb[:]),
                     reads=[R(pn)], writes=[R("resid")])
        ln_stage()

        for quad in range(4):
            a = "A" if quad % 2 == 0 else "B"
            load_arena(a, bt_r[quad, :, :])
            for j in range(2):
                oc = quad * 2 + j
                for t in range(2):
                    pb, pn = (p0, "p0") if (oc * 2 + t) % 2 == 0 else (p1, "p1")
                    for k in range(16):
                        P.op("tensor", lambda E, a=a, j=j, k=k, t=t, pb=pb: E.matmul(
                            pb[:], AR[a][:, k * 256 + j * 128:k * 256 + (j + 1) * 128],
                            xbf_s[:, k * 1024 + t * 512:k * 1024 + t * 512 + 512],
                            start=(k == 0), stop=(k == 15)),
                             reads=arena_res(a) + [R("xbf")], writes=[R(pn)])
                    P.op("scalar", lambda E, oc=oc, t=t, pb=pb: E.activation(
                        out=stb[:, oc * 1024 + t * 512:oc * 1024 + t * 512 + 512],
                        in_=pb[:], func=AF.Tanh),
                         reads=[R(pn)], writes=[R("stb")])
        for quad in range(4):
            a = "A" if quad % 2 == 0 else "B"
            load_arena(a, ct_r[quad, :, :])
            for j in range(4):
                oc = quad * 4 + j
                for t in range(2):
                    pb, pn = (p0, "p0") if (oc * 2 + t) % 2 == 0 else (p1, "p1")
                    for k in range(8):
                        P.op("tensor", lambda E, a=a, j=j, k=k, t=t, pb=pb: E.matmul(
                            pb[:], AR[a][:, k * 512 + j * 128:k * 512 + (j + 1) * 128],
                            stb[:, k * 1024 + t * 512:k * 1024 + t * 512 + 512],
                            start=(k == 0), stop=(k == 7)),
                             reads=arena_res(a) + [R("stb")], writes=[R(pn)])
                    c0 = oc * 1024 + t * 512
                    P.op("vector", lambda E, c0=c0, oc=oc, pb=pb: E.scalar_tensor_tensor(
                        h1t[:], resid[:, c0:c0 + 512], dv_sb[:, oc:oc + 1], pb[:],
                        ALU.mult, ALU.add),
                         reads=[R("resid"), R("dv"), R(pn)], writes=[R("h1t")])
                    P.op("vector", lambda E, c0=c0: E.tensor_add(resid[:, c0:c0 + 512],
                                                                 resid[:, c0:c0 + 512], h1t[:]),
                         reads=[R("h1t")], writes=[R("resid")])
        ln_stage()

        if pre_gu is not None:
            pre_gu(P)
        for t in range(2):
            for ocb in range(64):
                a = "A" if ocb % 2 == 0 else "B"
                pg, pgn = (p0, "p0") if ocb % 2 == 0 else (p2, "p2")
                pu, pun = (p1, "p1") if ocb % 2 == 0 else (p3, "p3")
                sq, sqn = (sq_bf, "sq") if ocb % 2 == 0 else (sq_bf2, "sq2")
                gu_load(a, 0, ocb)
                gu_load(a, 1, ocb + 64)
                for k in range(16):
                    P.op("tensor", lambda E, a=a, k=k, t=t, pg=pg: E.matmul(
                        pg[:], qslice(a, 0)[:, k * 128:(k + 1) * 128],
                        xbf_s[:, k * 1024 + t * 512:k * 1024 + t * 512 + 512],
                        start=(k == 0), stop=(k == 15)),
                         reads=[R(f"ws{a}0"), R("xbf")], writes=[R(pgn)])
                for k in range(16):
                    P.op("tensor", lambda E, a=a, k=k, t=t, pu=pu: E.matmul(
                        pu[:], qslice(a, 1)[:, k * 128:(k + 1) * 128],
                        xbf_s[:, k * 1024 + t * 512:k * 1024 + t * 512 + 512],
                        start=(k == 0), stop=(k == 15)),
                         reads=[R(f"ws{a}1"), R("xbf")], writes=[R(pun)])
                P.op("scalar", lambda E, pg=pg: E.activation(out=sgt[:], in_=pg[:], func=AF.Silu),
                     reads=[R(pgn)], writes=[R("sgt")])
                hc0 = ocb * 512
                P.op("vector", lambda E, hc0=hc0, pu=pu: E.tensor_mul(hbuf[:, hc0:hc0 + 512],
                                                                      sgt[:], pu[:]),
                     reads=[R("sgt"), R(pun)], writes=[R("hbuf")])
                P.op("scalar", lambda E, hc0=hc0, sq=sq: E.activation(out=sq[:],
                                                                      in_=hbuf[:, hc0:hc0 + 512],
                                                                      func=AF.Square),
                     reads=[R("hbuf")], writes=[R(sqn)])
                P.op("tensor", lambda E, ocb=ocb, sq=sq: E.matmul(
                    pS1[:], ones_sb[:], sq[:],
                    start=(ocb == 0), stop=(ocb == 63), skip_group_check=True),
                     reads=[R(sqn), R("ones")], writes=[R("pS1")])
            P.op("scalar", lambda E: E.activation(out=rowc[:], in_=pS1[:], func=AF.Ln,
                                                  scale=1.0 / 8192.0, bias=biasc[0:1, 1:2]),
                 reads=[R("pS1"), R("biasc")], writes=[R("rowc")])
            P.op("scalar", lambda E: E.activation(out=rowd[:], in_=rowc[:], func=AF.Exp,
                                                  scale=-0.5),
                 reads=[R("rowc")], writes=[R("rowd")])
            P.op("tensor", lambda E: E.matmul(p2[:], onesrf_sb[:], rowd[:], start=True, stop=True),
                 reads=[R("rowd"), R("onesrf")], writes=[R("p2")])
            for hc in range(64):
                hc0 = hc * 512
                P.op("vector", lambda E, hc0=hc0: E.tensor_mul(hbuf[:, hc0:hc0 + 512],
                                                               hbuf[:, hc0:hc0 + 512], p2[:]),
                     reads=[R("p2")], writes=[R("hbuf")])
            for oc in range(16):
                pr_load("A", oc, 0)
                pr_load("B", oc, 1)
                pb, pn = (p0, "p0") if oc % 2 == 0 else (p1, "p1")
                for hc in range(64):
                    a = "A" if hc < 32 else "B"
                    ho = hc % 32
                    P.op("tensor", lambda E, a=a, ho=ho, hc=hc, pb=pb: E.matmul(
                        pb[:], AR[a][:, ho * 128:(ho + 1) * 128],
                        hbuf[:, hc * 512:hc * 512 + 512],
                        start=(hc == 0), stop=(hc == 63)),
                         reads=[R(f"ws{a}{ho // 16}"), R("hbuf")], writes=[R(pn)])
                c0 = oc * 1024 + t * 512
                P.op("vector", lambda E, c0=c0, pb=pb: E.tensor_add(resid[:, c0:c0 + 512],
                                                                    resid[:, c0:c0 + 512], pb[:]),
                     reads=[R(pn)], writes=[R("resid")])
        ln_stage()

        out_toks = []
        for tt in range(8):
            for cs in range(4):
                for k4 in range(4):
                    k = cs * 4 + k4
                    P.op("tensor", lambda E, k=k, tt=tt: E.transpose(
                        pTf[:], resid[:, k * 1024 + tt * 128:k * 1024 + tt * 128 + 128],
                        idf_sb[:]),
                         reads=[R("resid"), R("idf")], writes=[R("pTf")])
                    P.op("vector", lambda E, k4=k4: E.tensor_copy(
                        xtile[:, k4 * 128:(k4 + 1) * 128], pTf[:]),
                         reads=[R("pTf")], writes=[R("xtile")])
                    tok = P.dma("sync", out[tt * 128:(tt + 1) * 128, cs * 512:(cs + 1) * 512],
                            xtile[:], reads=[R("xtile")])
                out_toks.append(tok)
        P.op("sync", lambda E: E.nop(), extra=out_toks)
        if ctx is None:
            P.finalize(st)
    return nc


def build_fused():
    from concourse.aot_env import allow_ici_all_to_all
    nc = bass.Bass("TRN2", target_bir_lowering=False)
    P = Prog(nc)
    rg8 = [list(range(NCORES))]
    with contextlib.ExitStack() as st:
        a2a_in = nc.dram_tensor("a2a_in", [2048, 1024], BF16, kind="Internal")
        a2a_out = nc.dram_tensor("a2a_out", [2048, 1024], BF16, kind="Internal")
        wgu_q_sh = nc.dram_tensor("wgu_q_sh", [256, 16384], BF16, kind="Internal")
        wpr_q_sh = nc.dram_tensor("wpr_q_sh", [1024, 2048], BF16, kind="Internal")
        wguq_nat = nc.dram_tensor("wguq_nat", [2048, 16384], BF16, kind="Internal")
        wprq_nat = nc.dram_tensor("wprq_nat", [8192, 2048], BF16, kind="Internal")
        cc_a = st.enter_context(nc.semaphore("cc_a"))
        cc_q1 = st.enter_context(nc.semaphore("cc_q1"))
        cc_q2 = st.enter_context(nc.semaphore("cc_q2"))

        ctx = {"nc": nc, "P": P,
               "wgu_q": wgu_q_sh.ap(), "wpr_q": wpr_q_sh.ap(),
               "q_rows": (256, 1024), "q_defer": True}
        build_Q(ctx)
        qg = ctx["q_steps"]

        def seg_hook(s):
            if ctx.get("q_done"):
                return
            n = 14 if s < NSEG else 10 ** 9
            for _ in range(n):
                v = next(qg, "done")
                if v == "done":
                    ctx["q_done"] = True
                    break
        ctx["seg_hook"] = seg_hook

        def q_tick():
            if not ctx.get("q_done"):
                if next(qg, "done") == "done":
                    ctx["q_done"] = True
        ctx["q_tick"] = q_tick

        def ag_fn(E):
            E.collective_compute(
                "AllGather", ALU.bypass, replica_groups=rg8,
                ins=[wgu_q_sh.ap().opt()], outs=[wguq_nat.ap().opt()],
            ).then_inc(cc_q1, 1)
            return E.collective_compute(
                "AllGather", ALU.bypass, replica_groups=rg8,
                ins=[wpr_q_sh.ap().opt()], outs=[wprq_nat.ap().opt()],
            ).then_inc(cc_q2, 1)
        ctx["o_dst"] = lambda s, cc: a2a_in.ap()[
            256 * s + 128 * cc:256 * s + 128 * (cc + 1), :]
        build_A(ctx)
        for _ in qg:
            pass
        P.op("gpsimd", ag_fn, extra=list(ctx["q_out_toks"]))

        def a2a_fn(E):
            with allow_ici_all_to_all():
                return E.collective_compute(
                    "AllToAll", ALU.bypass, replica_groups=rg8,
                    ins=[a2a_in.ap().opt()], outs=[a2a_out.ap().opt()],
                ).then_inc(cc_a, 1)
        P.op("gpsimd", a2a_fn, extra=list(ctx["a_out_toks"]))

        wgu_v = wguq_nat.ap().rearrange("(k p) c -> p k c", p=128)
        wpr_v = wprq_nat.ap().rearrange("(hc p) c -> p hc c", p=128)
        ctx["oT_src"] = a2a_out.ap().rearrange("(k p) c -> p k c", p=128)
        ctx["gu_src"] = lambda ocb: wgu_v[:, :, ocb * 128:(ocb + 1) * 128]
        ctx["pr_src"] = lambda oc, half: wpr_v[:, half * 32:(half + 1) * 32,
                                               oc * 128:(oc + 1) * 128]
        ctx["cgate"] = []
        ctx["pre_oT"] = lambda Pr: Pr.op(
            "gpsimd", lambda E: (E.wait_ge(cc_a, 1), E.nop())[1])
        ctx["pre_gu"] = lambda Pr: Pr.op(
            "sync", lambda E: (E.wait_ge(cc_q1, 1), E.wait_ge(cc_q2, 1), E.nop())[2])
        build_B(ctx)
        P.finalize(st)
    return nc


def _fused_inputs(inputs):
    amaps = _a_inputs(inputs)
    f32 = lambda a: np.asarray(a, dtype=np.float32)
    x = f32(inputs["L0_attn_hidden_states"]).reshape(T, HID)
    bmaps = _b_host_weights(inputs)
    qmaps = _q_inputs(inputs)
    maps = []
    for c in range(NCORES):
        m = {}
        m.update(amaps[c])
        m.update(bmaps)
        m.update(qmaps[c])
        m["x_sl"] = np.ascontiguousarray(x[1024 * c:1024 * (c + 1), :])
        maps.append(m)
    return maps


def _q_inputs(inputs):
    wguT = np.ascontiguousarray(np.asarray(inputs["L1_ffn_w_gate_up"], np.float32).T)
    wprT = np.ascontiguousarray(np.asarray(inputs["L1_ffn_w_proj"], np.float32).T)
    maps = []
    for c in range(NCORES):
        maps.append({
            "wgu_sh": np.ascontiguousarray(wguT[256 * c:256 * (c + 1), :]),
            "wpr_sh": np.ascontiguousarray(wprT[1024 * c:1024 * (c + 1), :]),
        })
    return maps


def _b_host_weights(inputs):
    f32 = lambda a: np.asarray(a, dtype=np.float32)
    wo_r = np.ascontiguousarray(f32(inputs["L0_attn_Wo"]).reshape(16, 128, 16, 128)
                                .transpose(2, 1, 0, 3).reshape(16, 128, 2048))
    rg_r = np.ascontiguousarray(f32(inputs["L0_ffn_router_gate"]).T.reshape(16, 128, 8)
                                .transpose(1, 0, 2).reshape(128, 128))
    egu = f32(inputs["L0_ffn_expert_gate_up"])
    egu_g = np.ascontiguousarray(
        egu[:, :, 0::2].reshape(8, 16, 128, 8, 128).transpose(0, 3, 2, 1, 4)
        .reshape(8, 8, 128, 2048))
    egu_u = np.ascontiguousarray(
        egu[:, :, 1::2].reshape(8, 16, 128, 8, 128).transpose(0, 3, 2, 1, 4)
        .reshape(8, 8, 128, 2048))
    edn = f32(inputs["L0_ffn_expert_down"])
    edn_r = np.ascontiguousarray(
        edn.reshape(8, 8, 128, 16, 128).transpose(3, 2, 0, 1, 4).reshape(16, 128, 8192))
    bt_r = np.ascontiguousarray(f32(inputs["L1_attn_B"]).T.reshape(16, 128, 4, 256)
                                .transpose(2, 1, 0, 3).reshape(4, 128, 4096))
    ct_r = np.ascontiguousarray(f32(inputs["L1_attn_C"]).T.reshape(8, 128, 4, 512)
                                .transpose(2, 1, 0, 3).reshape(4, 128, 4096))
    dvec = np.ascontiguousarray(f32(inputs["L1_attn_D"]).reshape(16, 128).T)
    return {"wo_r": wo_r, "rg_r": rg_r, "egu_g": egu_g, "egu_u": egu_u,
            "edn_r": edn_r, "bt_r": bt_r, "ct_r": ct_r, "dvec": dvec}


def _b_inputs(inputs, o_fm, wguqT, wprqT):
    f32 = lambda a: np.asarray(a, dtype=np.float32)
    x = f32(inputs["L0_attn_hidden_states"]).reshape(T, HID)
    w = _b_host_weights(inputs)
    wguq_P = np.ascontiguousarray(
        wguqT.reshape(16, 128, 128, 128).transpose(2, 1, 0, 3).reshape(128, 128, 2048))
    wprq_P = np.ascontiguousarray(
        wprqT.reshape(64, 128, 16, 128).transpose(2, 1, 0, 3).reshape(16, 128, 8192))
    maps = []
    for c in range(NCORES):
        ts = slice(1024 * c, 1024 * (c + 1))
        oT = np.ascontiguousarray(
            o_fm[:, ts].reshape(16, 128, 1024).transpose(1, 0, 2).reshape(128, 16 * 1024))
        m = {"x_sl": np.ascontiguousarray(x[ts, :]), "oT": oT,
             "wguq_P": wguq_P, "wprq_P": wprq_P}
        m.update(w)
        maps.append(m)
    return maps


_nc_cache = {}


def _get_nc(name, builder):
    if name not in _nc_cache:
        _nc_cache[name] = builder()
    return _nc_cache[name]


def _a_inputs(inputs):
    x = np.ascontiguousarray(np.asarray(inputs["L0_attn_hidden_states"],
                                        dtype=np.float32).reshape(T, HID))
    maps = []
    for c in range(NCORES):
        cs = slice(256 * c, 256 * c + 256)
        m = {
            "x": x,
            "wq": np.ascontiguousarray(inputs["L0_attn_Wq"][:, cs]),
            "wk": np.ascontiguousarray(inputs["L0_attn_Wk"][:, cs]),
            "wv": np.ascontiguousarray(inputs["L0_attn_Wv"][:, cs]),
            "ww": np.ascontiguousarray(inputs["L0_attn_Ww"][:, cs]),
            "cq": np.ascontiguousarray(
                inputs["L0_attn_cq"][cs, :].reshape(2, 128, 4).transpose(1, 0, 2).reshape(128, 8)),
            "ck": np.ascontiguousarray(
                inputs["L0_attn_ck"][cs, :].reshape(2, 128, 4).transpose(1, 0, 2).reshape(128, 8)),
            "cv": np.ascontiguousarray(
                inputs["L0_attn_cv"][cs, :].reshape(2, 128, 4).transpose(1, 0, 2).reshape(128, 8)),
            "wf1": np.ascontiguousarray(inputs["L0_attn_Wf1"]),
            "wf2": np.ascontiguousarray(inputs["L0_attn_Wf2"][:, cs]),
            "wg1": np.ascontiguousarray(inputs["L0_attn_Wg1"]),
            "wg2": np.ascontiguousarray(inputs["L0_attn_Wg2"][:, cs]),
            "wbp": np.concatenate(
                [np.asarray(inputs["L0_attn_Wb"][:, 2 * c + j]).reshape(16, 128).T
                 for j in (0, 1)], axis=1),
            "bg2": np.ascontiguousarray(
                np.asarray(inputs["L0_attn_bg2"][cs]).reshape(2, 128).T),
            "normw": np.ascontiguousarray(inputs["L0_attn_norm_w"].reshape(128, 1)),
        }
        maps.append({k: np.asarray(v, dtype=np.float32) for k, v in m.items()})
    return maps


def run_launch_A(inputs):
    nc = _get_nc("A", build_A)
    maps = _a_inputs(inputs)
    res = run_bass_kernel_spmd(nc, maps, core_ids=list(range(NCORES)))
    o = np.concatenate([res.results[c]["o_fm"] for c in range(NCORES)], axis=0)
    return o


def run_launch_Q(inputs):
    nc = _get_nc("Q", build_Q)
    maps = _q_inputs(inputs)
    res = run_bass_kernel_spmd(nc, maps, core_ids=list(range(NCORES)))
    wguqT = np.concatenate([res.results[c]["wgu_q"] for c in range(NCORES)], axis=0)
    wprqT = np.concatenate([res.results[c]["wpr_q"] for c in range(NCORES)], axis=0)
    return wguqT, wprqT


def run_launch_B(inputs, o_fm, wguqT, wprqT):
    nc = _get_nc("B", build_B)
    maps = _b_inputs(inputs, o_fm, wguqT, wprqT)
    res = run_bass_kernel_spmd(nc, maps, core_ids=list(range(NCORES)))
    return np.concatenate([res.results[c]["out"] for c in range(NCORES)], axis=0)


FUSED = True


def kernel(**inputs):
    if FUSED:
        nc = _get_nc("F", build_fused)
        maps = _fused_inputs(inputs)
        res = run_bass_kernel_spmd(nc, maps, core_ids=list(range(NCORES)))
        y = np.concatenate([res.results[c]["out"] for c in range(NCORES)], axis=0)
        return y.reshape(1, T, HID).astype(np.float32)
    o_fm = run_launch_A(inputs)
    wguqT, wprqT = run_launch_Q(inputs)
    y = run_launch_B(inputs, o_fm, wguqT, wprqT)
    return y.reshape(1, T, HID).astype(np.float32)
